# version 19
# baseline (speedup 1.0000x reference)
import contextlib
import numpy as np
import concourse.bass as bass
import concourse.mybir as mybir
from concourse.bass_utils import run_bass_kernel_spmd

F32 = mybir.dt.float32
BF16 = mybir.dt.bfloat16
AF = mybir.ActivationFunctionType
ALU = mybir.AluOpType
AX = mybir.AxisListType.X

T = 2048
H = 1024
DXZ = 4096
D = 2048
NST = 16
RNK = 64
II = 2816
NHC = H // 128
NDC = D // 128
NIC = II // 128
NTC = T // 128
TCF = 512
NCF = T // TCF
TCM = 256
NCM = T // TCM
GAP = TCM + 1
WSC = NST * GAP
MAGIC = 12582912.0
EPS = 1e-5
QMAX = 127.0
RG = [[0, 1, 2, 3, 4, 5, 6, 7]]

_STOP = None


class Ctr:
    def __init__(self, sem):
        self.sem = sem
        self.v = 0


class Prog:
    ENGINES = ("sync", "gpsimd", "vector", "scalar", "tensor")

    def __init__(self):
        self.streams = {e: [] for e in self.ENGINES}

    def add(self, eng, fn, wait=(), inc=None, inc_k=1):
        waits = [(c.sem, v) for (c, v) in wait if v > 0]
        if inc is not None:
            inc.v += inc_k
        sem = inc.sem if inc is not None else None

        def run(e, fn=fn, waits=waits, sem=sem, inc_k=inc_k):
            for s, v in waits:
                e.wait_ge(s, v)
            ins = fn(e)
            if sem is not None:
                ins.then_inc(sem, inc_k)
        self.streams[eng].append(run)

    def wait_only(self, eng, wait):
        waits = [(c.sem, v) for (c, v) in wait if v > 0]
        if not waits:
            return

        def run(e, waits=waits):
            for s, v in waits:
                e.wait_ge(s, v)
        self.streams[eng].append(run)


def build_nc(stop=None):
    nc = bass.Bass("TRN2", target_bir_lowering=False)

    def ein(name, shape, dtype=F32):
        return nc.dram_tensor(name, shape, dtype, kind="ExternalInput").ap()

    x_in = ein("x", [T, H])
    winT2 = ein("winT2", [DXZ, H])
    wx_r = ein("wx_r", [128, NDC * 96])
    wdtT = ein("wdtT", [RNK, D])
    woutT = ein("woutT", [D, H])
    wg_n = ein("wg_n", [II, H])
    wgT = ein("wgT", [H, II])
    wu_n = ein("wu_n", [II, H])
    wuT = ein("wuT", [H, II])
    wd_n = ein("wd_n", [H, II])
    wdT = ein("wdT", [II, H])
    convw = ein("convw", [128, NDC * 4])
    convb = ein("convb", [128, NDC])
    bdt = ein("bdt", [128, NDC])
    alog = ein("alog", [128, NDC * NST])
    dpar = ein("dpar", [128, NDC])
    clip2 = ein("clip2", [1, 2])

    o_out = nc.dram_tensor("o", [T, H], F32, kind="ExternalOutput").ap()

    s_winbf2 = nc.dram_tensor("s_winbf2", [DXZ, H], BF16, kind="Internal").ap()
    s_woutbf = nc.dram_tensor("s_woutbf", [D, H], BF16, kind="Internal").ap()
    s_wgq = nc.dram_tensor("s_wgq", [H, II], BF16, kind="Internal").ap()
    s_wuq = nc.dram_tensor("s_wuq", [H, II], BF16, kind="Internal").ap()
    s_wdq = nc.dram_tensor("s_wdq", [II, H], BF16, kind="Internal").ap()
    s_h = nc.dram_tensor("s_h", [NIC, NCF, 128, TCF], F32,
                         kind="Internal").ap()
    s_x1 = nc.dram_tensor("s_x1", [T, H], F32, kind="Internal").ap()
    s_x2 = nc.dram_tensor("s_x2", [T, H], F32, kind="Internal").ap()
    s_xln2 = nc.dram_tensor("s_xln2", [NHC, NTC, 128, 128], BF16,
                            kind="Internal").ap()
    gmi = [nc.dram_tensor(f"gmi{i}", [1, 1], F32, kind="Internal").ap()
           for i in range(4)]
    gmo = [nc.dram_tensor(f"gmo{i}", [1, 1], F32, kind="Internal").ap()
           for i in range(4)]

    P = Prog()
    root = contextlib.ExitStack()

    def mk(stk, name, shape, dtype=F32):
        return stk.enter_context(nc.sbuf_tensor(name, shape, dtype))

    def mkp(stk, name, shape, dtype=F32):
        return stk.enter_context(nc.psum_tensor(name, shape, dtype))

    ident = mk(root, "ident", [128, 128], F32)
    identb = mk(root, "identb", [128, 128], BF16)
    ones1 = mk(root, "ones1", [1, 128], F32)
    convw_t = mk(root, "convw_t", [128, NDC * 4])
    convb_t = mk(root, "convb_t", [128, NDC])
    bdt_t = mk(root, "bdt_t", [128, NDC])
    acol = mk(root, "acol", [128, NDC * NST])
    dpar_t = mk(root, "dpar_t", [128, NDC])
    wx_b = mk(root, "wx_b", [128, NDC * 96], BF16)
    wdt_b = mk(root, "wdt_b", [RNK, D], BF16)
    clip_t = mk(root, "clip_t", [128, 2])
    rstdA = mk(root, "rstdA", [128, NTC])
    nmrA = mk(root, "nmrA", [128, NTC])
    rstdB = mk(root, "rstdB", [128, NTC])
    nmrB = mk(root, "nmrB", [128, NTC])
    rstdC = mk(root, "rstdC", [128, NTC])
    nmrC = mk(root, "nmrC", [128, NTC])
    sumS = mk(root, "sumS", [128, NTC])
    sqS = mk(root, "sqS", [128, NTC])
    scS = mk(root, "scS", [128, NTC])
    biS = mk(root, "biS", [128, NTC])
    lmax = mk(root, "lmax", [128, NIC])
    col1 = mk(root, "col1", [128, 1])
    col2 = mk(root, "col2", [128, 1])
    grow = mk(root, "grow", [1, 128])
    gsb = mk(root, "gsb", [1, 1])
    sx_rep = mk(root, "sx_rep", [128, 1])
    isx_rep = mk(root, "isx_rep", [128, 1])
    sh_rep = mk(root, "sh_rep", [128, 1])
    ish_rep = mk(root, "ish_rep", [128, 1])
    sg_col = mk(root, "sg_col", [128, NIC])
    su_col = mk(root, "su_col", [128, NIC])
    sd_col = mk(root, "sd_col", [128, NHC])
    sxg_col = mk(root, "sxg_col", [128, NIC])
    sxu_col = mk(root, "sxu_col", [128, NIC])
    hq_col = mk(root, "hq_col", [128, NIC])
    dn_col = mk(root, "dn_col", [128, NHC])
    xt01 = [mk(root, f"xt{i}", [128, H]) for i in range(2)]
    qa = mk(root, "qa", [128, H])
    wrkA = mk(root, "wrkA", [128, TCF])

    pA = mkp(root, "pA", [128, 512], F32)
    pB = mkp(root, "pB", [128, 512], F32)
    pM = mkp(root, "pM", [128, 512], F32)
    pT1 = mkp(root, "pT1", [128, 512], F32)

    CV = Ctr(root.enter_context(nc.semaphore("CV")))
    CS = Ctr(root.enter_context(nc.semaphore("CS")))
    CT = Ctr(root.enter_context(nc.semaphore("CT")))
    CG = Ctr(root.enter_context(nc.semaphore("CG")))
    CC = Ctr(root.enter_context(nc.semaphore("CC")))
    ALLC = [CV, CS, CT, CG, CC]
    _dsem = {}

    def DS(name):
        if name not in _dsem:
            _dsem[name] = Ctr(root.enter_context(nc.semaphore("D_" + name)))
            ALLC.append(_dsem[name])
        return _dsem[name]

    last_use = {}

    def lu(buf):
        return list(last_use.get(id(buf), []))

    def setlu(buf, ev):
        last_use[id(buf)] = [ev]

    def addlu(buf, ev):
        last_use.setdefault(id(buf), []).append(ev)

    def dma_in(dst, src, stream="ld", wait=(), eng="sync"):
        c = DS("i" + stream)
        P.add(eng, lambda e, d=dst, s=src: e.dma_start(out=d, in_=s),
              wait=tuple(wait) + ((c, c.v),), inc=c, inc_k=16)
        return (c, c.v)

    def dma_out(dst, src, stream="st", wait=(), eng="gpsimd"):
        c = DS("o" + stream)
        P.add(eng, lambda e, d=dst, s=src: e.dma_start(out=d, in_=s),
              wait=tuple(wait) + ((c, c.v),), inc=c, inc_k=16)
        return (c, c.v)

    def barrier():
        vals = [(c, c.v) for c in ALLC]
        for e in Prog.ENGINES:
            P.wait_only(e, vals)

    def V(fn, wait=()):
        P.add("vector", fn, wait=tuple(wait) + ((CV, CV.v),), inc=CV)
        return (CV, CV.v)

    def S(fn, wait=()):
        P.add("scalar", fn, wait=tuple(wait) + ((CS, CS.v),), inc=CS)
        return (CS, CS.v)

    def TE(fn, wait=()):
        P.add("tensor", fn, wait=wait, inc=CT)
        return (CT, CT.v)

    def G(fn, wait=()):
        P.add("gpsimd", fn, wait=tuple(wait) + ((CG, CG.v),), inc=CG)
        return (CG, CG.v)

    if stop == "smoke":
        smk = mk(root, "smk", [128, H])
        sm_sem = Ctr(root.enter_context(nc.semaphore("SMK")))
        for tc in range(NTC):
            P.add("sync", lambda e, tc=tc: e.dma_start(
                out=smk[:], in_=x_in[tc * 128:(tc + 1) * 128, :]),
                wait=[(sm_sem, 32 * tc)], inc=sm_sem, inc_k=16)
            P.add("sync", lambda e, tc=tc: e.dma_start(
                out=o_out[tc * 128:(tc + 1) * 128, :], in_=smk[:]),
                wait=[(sm_sem, 32 * tc + 16)], inc=sm_sem, inc_k=16)
        P.wait_only("sync", [(sm_sem, 32 * NTC)])
        root.close()
        return nc, P

    for dst, src in ((convw_t, convw), (convb_t, convb), (bdt_t, bdt),
                     (acol, alog), (dpar_t, dpar)):
        ep = dma_in(dst[:], src, "par")

    G(lambda g: g.memset(ident[:], 0.0))
    G(lambda g: g.affine_select(
        out=ident[:], in_=ident[:], compare_op=ALU.not_equal, fill=1.0,
        base=0, pattern=[[-1, 128]], channel_multiplier=1))
    G(lambda g: g.memset(identb[:], 0.0))
    G(lambda g: g.affine_select(
        out=identb[:], in_=identb[:], compare_op=ALU.not_equal, fill=1.0,
        base=0, pattern=[[-1, 128]], channel_multiplier=1))
    e_setup = G(lambda g: g.memset(ones1[:], 1.0))

    e_ac = S(lambda e: e.activation(acol[:], acol[:], AF.Exp), wait=[ep])
    V(lambda e: e.tensor_scalar_mul(acol[:], acol[:], -1.0), wait=[e_ac])

    e_cl = dma_in(gsb[:], clip2[0:1, 0:1], "par")
    e_t = TE(lambda e: e.matmul(pM[:, 0:1], ones1[:], gsb[:],
                                start=True, stop=True), wait=[e_cl, e_setup])
    e_s = S(lambda e: e.activation(clip_t[:, 0:1], pM[:, 0:1], AF.Sigmoid),
            wait=[e_t])
    e_cl = dma_in(gsb[:], clip2[0:1, 1:2], "par", wait=[e_t])
    e_t = TE(lambda e: e.matmul(pM[:, 0:1], ones1[:], gsb[:],
                                start=True, stop=True), wait=[e_cl, e_s])
    e_clipd = S(lambda e: e.activation(clip_t[:, 1:2], pM[:, 0:1], AF.Sigmoid),
                wait=[e_t])
    setlu(pM, e_clipd)
    setlu(gsb, e_t)

    def ln_stats_tile(tile, tc, rstd_c, nmr_c, with_max, lmx, w0=()):
        e_sqr = S(lambda e, t=tile, tc=tc: e.activation(
            qa[:], t[:], AF.Square, accum_out=sqS[:, tc:tc + 1]),
            wait=lu(qa) + list(w0))
        setlu(qa, e_sqr)
        V(lambda e, t=tile, tc=tc: e.reduce_sum(sumS[:, tc:tc + 1], t[:], AX),
          wait=list(w0))
        V(lambda e, tc=tc: e.tensor_scalar_mul(
            sumS[:, tc:tc + 1], sumS[:, tc:tc + 1], 1.0 / H))
        V(lambda e, tc=tc: e.tensor_tensor(
            col1[:], sumS[:, tc:tc + 1], sumS[:, tc:tc + 1], ALU.mult))
        e_v = V(lambda e, tc=tc: e.scalar_tensor_tensor(
            col1[:], sqS[:, tc:tc + 1], 1.0 / H, col1[:],
            ALU.mult, ALU.subtract), wait=[e_sqr])
        e_v = V(lambda e: e.tensor_scalar(col1[:], col1[:], EPS, None,
                                          ALU.add), wait=[e_v])
        e_sq = S(lambda e: e.activation(col2[:], col1[:], AF.Sqrt),
                 wait=[e_v])
        V(lambda e, r=rstd_c, tc=tc: e.reciprocal(r[:, tc:tc + 1], col2[:]),
          wait=[e_sq])
        e_n = V(lambda e, r=rstd_c, m=nmr_c, tc=tc: e.scalar_tensor_tensor(
            m[:, tc:tc + 1], sumS[:, tc:tc + 1], -1.0, r[:, tc:tc + 1],
            ALU.mult, ALU.mult))
        if with_max:
            e_a = S(lambda e, t=tile, r=rstd_c, m=nmr_c, tc=tc: e.activation(
                qa[:], t[:], AF.Identity, bias=m[:, tc:tc + 1],
                scale=r[:, tc:tc + 1]), wait=[e_n] + lu(qa))
            e_n = V(lambda e, tc=tc, lx=lmx: e.reduce_max(
                lx[:, tc:tc + 1], qa[:], AX, apply_absolute_value=True),
                wait=[e_a])
            setlu(qa, e_n)
        return e_n

    def global_max(k, ncols, s_rep, is_rep, sig_idx, lmx):
        e1 = V(lambda e, n=ncols, lx=lmx: e.reduce_max(col1[:], lx[:, 0:n], AX))
        e2 = TE(lambda e: e.transpose(pT1[0:1, 0:128], col1[:], ident[:]),
                wait=[e1] + lu(pT1))
        e3 = V(lambda e: e.tensor_copy(grow[:], pT1[0:1, 0:128]), wait=[e2])
        setlu(pT1, e3)
        e4 = V(lambda e: e.reduce_max(gsb[:], grow[:], AX),
               wait=[e3] + lu(gsb))
        e5 = dma_out(gmi[k], gsb[:], "gm", wait=[e4])
        P.add("gpsimd", lambda e, k=k: e.collective_compute(
            "AllReduce", ALU.max, replica_groups=RG,
            ins=[gmi[k].opt()], outs=[gmo[k].opt()]), wait=[e5], inc=CC)
        e6 = (CC, CC.v)
        e7 = dma_in(gsb[:], gmo[k], "gm", wait=[e6])
        setlu(gsb, e7)
        e8 = TE(lambda e: e.matmul(pM[:, 0:1], ones1[:], gsb[:],
                                   start=True, stop=True),
                wait=[e7] + lu(pM))
        setlu(gsb, e8)
        e9 = V(lambda e, si=sig_idx, sr=s_rep: e.scalar_tensor_tensor(
            sr[:], pM[:, 0:1], 1.0 / QMAX, clip_t[:, si:si + 1],
            ALU.mult, ALU.mult), wait=[e8, e_clipd])
        setlu(pM, e9)
        e10 = V(lambda e, sr=s_rep, ir=is_rep: e.reciprocal(ir[:], sr[:]),
                wait=[e9])
        return e10

    stk0 = contextlib.ExitStack()
    wide_f = [mk(stk0, f"widef{i}", [128, II]) for i in range(2)]
    wide_b = [mk(stk0, f"wideb{i}", [128, II], BF16) for i in range(2)]

    def cast_spill(src, dst, nrows, cols):
        pieces = [(c0, min(cols, c0 + 2048)) for c0 in range(0, cols, 2048)]
        for r in range(nrows // 128):
            for c0, c1 in pieces:
                fs, bs = wide_f[r % 2], wide_b[r % 2]
                el = dma_in(fs[:, 0:c1 - c0],
                            src[r * 128:(r + 1) * 128, c0:c1],
                            "wf%d" % (r % 2), wait=lu(fs))
                ec = V(lambda e, f=fs, b=bs, c=c1 - c0: e.tensor_copy(
                    b[:, 0:c], f[:, 0:c]), wait=[el] + lu(bs))
                setlu(fs, ec)
                es = dma_out(dst[r * 128:(r + 1) * 128, c0:c1],
                             bs[:, 0:c1 - c0], "wb%d" % (r % 2), wait=[ec])
                setlu(bs, es)

    cast_spill(winT2, s_winbf2, DXZ, H)
    cast_spill(woutT, s_woutbf, D, H)
    el = dma_in(wide_f[0][:, 0:NDC * 96], wx_r, "wf0", wait=lu(wide_f[0]))
    ec = V(lambda e: e.tensor_copy(wx_b[:], wide_f[0][:, 0:NDC * 96]),
           wait=[el])
    setlu(wide_f[0], ec)
    el = dma_in(wide_f[1][0:RNK, 0:D], wdtT, "wf1", wait=lu(wide_f[1]))
    ec = V(lambda e: e.tensor_copy(wdt_b[:], wide_f[1][0:RNK, 0:D]), wait=[el])
    setlu(wide_f[1], ec)

    rq_row = mk(stk0, "rq_row", [1, NIC * 128])
    rqT_sb = mk(stk0, "rqT_sb", [NIC, 128])
    rq_bc = mk(stk0, "rq_bc", [128, II])
    amax_c = mk(stk0, "amax_c", [128, NIC])

    wq_pro = mk(stk0, "wq_pro", [128, II], BF16)

    def quant_w(w_nat, w_T, NRC, ncols, scale_col, spill, hook=None):
        for ic in range(NRC):
            fs = wide_f[ic % 2]
            el = dma_in(fs[:, 0:ncols], w_nat[ic * 128:(ic + 1) * 128, :],
                        "wf%d" % (ic % 2), wait=lu(fs))
            if hook is not None and ic < NTC:
                hook(ic)
            ec = V(lambda e, f=fs, ic=ic, nc_=ncols: e.reduce_max(
                amax_c[:, ic:ic + 1], f[:, 0:nc_], AX,
                apply_absolute_value=True), wait=[el])
            setlu(fs, ec)
        V(lambda e, n=NRC: e.tensor_scalar_mul(scale_col[:, 0:n],
                                               amax_c[:, 0:n], 1.0 / QMAX))
        V(lambda e, n=NRC: e.reciprocal(amax_c[:, 0:n], amax_c[:, 0:n]))
        e_rq = V(lambda e, n=NRC: e.tensor_scalar_mul(amax_c[:, 0:n],
                                                      amax_c[:, 0:n], QMAX))
        e_t = TE(lambda e, n=NRC: e.transpose(pT1[0:n, 0:128],
                                              amax_c[:, 0:n], ident[:]),
                 wait=[e_rq, e_setup] + lu(pT1))
        e_v = V(lambda e, n=NRC: e.tensor_copy(rqT_sb[0:n, :],
                                               pT1[0:n, 0:128]),
                wait=[e_t] + lu(rqT_sb))
        setlu(pT1, e_v)
        e_d = dma_out(rq_row[0:1, 0:NRC * 128], rqT_sb[0:NRC, :], "rq",
                      wait=[e_v] + lu(rq_row))
        setlu(rqT_sb, e_d)
        wid = NRC * 128
        e_bc = None
        for j in range((wid + 511) // 512):
            c0, c1 = j * 512, min(wid, (j + 1) * 512)
            e_t = TE(lambda e, c0=c0, c1=c1: e.matmul(
                pM[:, 0:c1 - c0], ones1[:], rq_row[0:1, c0:c1],
                start=True, stop=True), wait=[e_d] + lu(pM))
            setlu(rq_row, e_t)
            e_bc = V(lambda e, c0=c0, c1=c1: e.tensor_copy(
                rq_bc[:, c0:c1], pM[:, 0:c1 - c0]), wait=[e_t])
            setlu(pM, e_bc)
        for hc in range(ncols // 128):
            fs = wide_f[hc % 2]
            el = dma_in(fs[:, 0:wid], w_T[hc * 128:(hc + 1) * 128, 0:wid],
                        "wf%d" % (hc % 2), wait=lu(fs))
            V(lambda e, f=fs, w=wid: e.tensor_tensor(
                f[:, 0:w], f[:, 0:w], rq_bc[:, 0:w], ALU.mult),
                wait=[el, e_bc])
            V(lambda e, f=fs, w=wid: e.tensor_scalar(
                f[:, 0:w], f[:, 0:w], MAGIC, MAGIC, ALU.add, ALU.subtract))
            ec = V(lambda e, f=fs, w=wid: e.tensor_copy(
                wq_pro[:, 0:w], f[:, 0:w]), wait=lu(wq_pro))
            setlu(fs, ec)
            es = dma_out(spill[hc * 128:(hc + 1) * 128, 0:wid],
                         wq_pro[:, 0:wid], "wq", wait=[ec])
            setlu(wq_pro, es)

    def ln_tile(tc):
        tile = xt01[tc % 2]
        el = dma_in(tile[:], x_in[tc * 128:(tc + 1) * 128, :],
                    "x%d" % (tc % 2), wait=lu(tile), eng="scalar")
        e_n0 = ln_stats_tile(tile, tc, rstdA, nmrA, True, lmax, w0=[el])
        setlu(tile, e_n0)

    quant_w(wg_n, wgT, NIC, H, sg_col, s_wgq,
            hook=(ln_tile if stop != "smoke" else None))
    quant_w(wu_n, wuT, NIC, H, su_col, s_wuq)
    e_spills_gu = (DS("owq"), DS("owq").v)
    quant_w(wd_n, wdT, NHC, II, sd_col, s_wdq)
    e_spills_all = (DS("owq"), DS("owq").v)
    barrier()
    stk0.close()
    if stop == "pro":
        P.wait_only("sync", [(c, c.v) for c in ALLC])
        root.close()
        return nc, P

    ffn_s = contextlib.ExitStack()
    xq = mk(ffn_s, "xq", [128, NHC * T], BF16)
    xqt1 = [mk(ffn_s, f"xqt1_{i}", [128, H], BF16) for i in range(2)]
    stg1 = [mk(ffn_s, f"stg1_{i}", [128, TCF], BF16) for i in range(2)]
    xtf1 = xt01 + [mk(ffn_s, f"xtf1_{i}", [128, H]) for i in range(2)]
    wrkB1 = mk(ffn_s, "wrkB1", [128, TCF])
    wrkC1 = mk(ffn_s, "wrkC1", [128, TCF])
    wrkD1 = mk(ffn_s, "wrkD1", [128, TCF])
    pBF1 = mkp(ffn_s, "pBF1", [128, 2048], BF16)
    pT21 = mkp(ffn_s, "pT21", [128, 512], F32)

    def load_wgu(stk, tag):
        wgq_ = mk(stk, "wgq" + tag, [128, NHC * II], BF16)
        wuq_ = mk(stk, "wuq" + tag, [128, NHC * II], BF16)
        e_wl = None
        for hc in range(NHC):
            dma_in(wgq_[:, hc * II:(hc + 1) * II],
                   s_wgq[hc * 128:(hc + 1) * 128, :], "lg",
                   wait=[e_spills_gu] if hc == 0 else ())
            e_wl0 = dma_in(wuq_[:, hc * II:(hc + 1) * II],
                           s_wuq[hc * 128:(hc + 1) * 128, :], "lu",
                           wait=[e_spills_gu] if hc == 0 else ())
            e_wl = (DS("ilg"), DS("ilg").v) if False else e_wl0
        P.wait_only("tensor", [e_wl, (DS("ilg"), DS("ilg").v)])
        return wgq_, wuq_

    wstk1 = contextlib.ExitStack()
    wgq, wuq = load_wgu(wstk1, "1")

    def ffn_front(x_src, k0, k1, stats, wgq_, wuq_, xq_, xqt, pBF,
                  wrkB, wrkC, wrkD):
        bank_ev = {}
        rstd_c, nmr_c = stats
        if rstd_c is None:
            rstd_c, nmr_c = rstdA, nmrA
            for tc in range(NTC):
                tile = xt01[tc % 2]
                el = dma_in(tile[:], x_src[tc * 128:(tc + 1) * 128, :],
                            "x%d" % (tc % 2), wait=lu(tile), eng="scalar")
                P.wait_only("scalar", [el])
                P.wait_only("vector", [el])
                e_n = ln_stats_tile(tile, tc, rstd_c, nmr_c, True, lmax)
                setlu(tile, e_n)
        if stop == "stats":
            return None, None
        e_is = global_max(k0, NTC, sx_rep, isx_rep, 0, lmax)
        if stop == "gmax":
            return None, None
        V(lambda e: e.tensor_scalar(sxg_col[:, 0:NIC], sg_col[:, 0:NIC],
                                    sx_rep[:, 0:1], None, ALU.mult),
          wait=[e_is])
        V(lambda e: e.tensor_scalar(sxu_col[:, 0:NIC], su_col[:, 0:NIC],
                                    sx_rep[:, 0:1], None, ALU.mult))
        V(lambda e, r=rstd_c: e.tensor_scalar(
            scS[:, 0:NTC], r[:, 0:NTC], isx_rep[:, 0:1], None, ALU.mult))
        e_sc = V(lambda e, m=nmr_c: e.tensor_scalar(
            biS[:, 0:NTC], m[:, 0:NTC], isx_rep[:, 0:1], None, ALU.mult))
        for tc in range(NTC):
            tile = xt01[tc % 2]
            bt = xqt[tc % 2]
            el = dma_in(tile[:], x_src[tc * 128:(tc + 1) * 128, :],
                        "x%d" % (tc % 2), wait=lu(tile), eng="scalar")
            e_a = S(lambda e, t=tile, tc=tc: e.activation(
                qa[:], t[:], AF.Identity, bias=biS[:, tc:tc + 1],
                scale=scS[:, tc:tc + 1]), wait=[el, e_sc] + lu(qa))
            setlu(tile, e_a)
            e_m = V(lambda e: e.tensor_scalar(qa[:], qa[:], MAGIC, MAGIC,
                                              ALU.add, ALU.subtract),
                    wait=[e_a] + lu(bt))
            e_q = V(lambda e, b=bt: e.tensor_scalar(
                b[:], qa[:], -QMAX - 1.0, QMAX, ALU.max, ALU.min))
            setlu(qa, e_q)
            bank = (tc % 2) * 1024
            e_t = None
            for hc in range(NHC):
                wv = [e_q]
                if hc == 0 and bank in bank_ev:
                    wv.append(bank_ev[bank])
                e_t = TE(lambda e, b=bt, hc=hc, sl=bank + hc * 128: e.transpose(
                    pBF[:, sl:sl + 128], b[:, hc * 128:(hc + 1) * 128],
                    identb[:]), wait=wv)
            e_c = None
            for hc in range(NHC):
                e_c = V(lambda e, hc=hc, tc=tc, sl=bank + hc * 128, xq_=xq_:
                        e.tensor_copy(
                    xq_[:, hc * T + tc * 128:hc * T + (tc + 1) * 128],
                    pBF[:, sl:sl + 128]), wait=[e_t] if hc == 0 else ())
            bank_ev[bank] = e_c
            setlu(bt, e_c)
        e_xq = (CV, CV.v)
        if stop == "quant":
            return None, None
        V(lambda e: e.memset(lmax[:, 0:NIC], 0.0))
        e_sl = None
        e_b = None
        k = 0
        for tch in range(NCF):
            for ic in range(NIC):
                pg = [pA, pM][k % 2]
                pu = [pB, pT1][k % 2]
                wa = [wrkA, wrkC][k % 2]
                wb = [wrkB, wrkD][k % 2]
                k += 1
                wv = ([e_xq] if (tch == 0 and ic == 0) else []) + lu(pg)
                e_mg = None
                for hc in range(NHC):
                    e_mg = TE(lambda e, ic=ic, hc=hc, tch=tch,
                              w_=wgq_, x_=xq_, pg=pg: e.matmul(
                        pg[:, 0:512],
                        w_[:, hc * II + ic * 128:hc * II + (ic + 1) * 128],
                        x_[:, hc * T + tch * TCF:hc * T + tch * TCF + TCF],
                        start=(hc == 0), stop=(hc == NHC - 1)),
                        wait=wv if hc == 0 else ())
                wv = lu(pu)
                e_mu = None
                for hc in range(NHC):
                    e_mu = TE(lambda e, ic=ic, hc=hc, tch=tch,
                              w_=wuq_, x_=xq_, pu=pu: e.matmul(
                        pu[:, 0:512],
                        w_[:, hc * II + ic * 128:hc * II + (ic + 1) * 128],
                        x_[:, hc * T + tch * TCF:hc * T + tch * TCF + TCF],
                        start=(hc == 0), stop=(hc == NHC - 1)),
                        wait=wv if hc == 0 else ())
                e_sl = S(lambda e, ic=ic, pg=pg, wa=wa: e.activation(
                    wa[:], pg[:, 0:512], AF.Silu,
                    scale=sxg_col[:, ic:ic + 1]), wait=[e_mg] + lu(wa))
                setlu(pg, e_sl)
                e_b = V(lambda e, pu=pu, wa=wa, wb=wb: e.tensor_tensor(
                    wb[:], wa[:], pu[:, 0:512], ALU.mult),
                    wait=[e_sl, e_mu] + lu(wb))
                setlu(pu, e_b)
                setlu(wa, e_b)
                V(lambda e, wb=wb: e.reduce_max(col1[:], wb[:], AX,
                                                apply_absolute_value=True))
                V(lambda e, ic=ic: e.tensor_tensor(
                    lmax[:, ic:ic + 1], lmax[:, ic:ic + 1], col1[:], ALU.max))
                e_sp = dma_out(s_h[ic, tch], wb[:],
                               "sh%d" % (k % 2), wait=[e_b])
                setlu(wb, e_sp)
        if stop == "gateup":
            return None, None
        e_l = V(lambda e: e.tensor_tensor(lmax[:, 0:NIC], lmax[:, 0:NIC],
                                          sxu_col[:, 0:NIC], ALU.mult))
        e_ish = global_max(k1, NIC, sh_rep, ish_rep, 1, lmax)
        e_hq = V(lambda e: e.tensor_scalar(hq_col[:, 0:NIC], sxu_col[:, 0:NIC],
                                           ish_rep[:, 0:1], None, ALU.mult),
                 wait=[e_ish])
        e_dn = V(lambda e: e.tensor_scalar(
            dn_col[:, 0:NHC], sd_col[:, 0:NHC], 0.5, sh_rep[:, 0:1],
            ALU.mult, ALU.mult), wait=[e_ish])
        return e_hq, e_dn

    def ffn_down(x_src, dst, e_hq, e_dn, final, stk, xqt, stg_b, pBF,
                 pT2, xt):
        bank_ev = {}
        tag = "F" if final else "A"
        wdq = mk(stk, "wdq" + tag, [128, NIC * H], BF16)
        hq_sb2 = [mk(stk, f"hq_sb{tag}{i}", [128, NIC * TCF], BF16)
                  for i in range(2)]
        hstg = [mk(stk, f"hstg{tag}{i}", [128, TCF]) for i in range(2)]
        e_wd = None
        for ic in range(NIC):
            e_wd = dma_in(wdq[:, ic * H:(ic + 1) * H],
                          s_wdq[ic * 128:(ic + 1) * 128, :], "wd",
                          wait=[e_spills_all] if ic == 0 else ())
        e_add_prev = None
        hqs_ev = {}

        def q_stage(tch):
            hq_sb = hq_sb2[tch % 2]
            e_hqs = None
            for ic in range(NIC):
                hs = hstg[ic % 2]
                el = dma_in(hs[:], s_h[ic, tch],
                            "h%d" % (ic % 2), wait=lu(hs), eng="scalar")
                V(lambda e, h=hs, ic=ic: e.tensor_scalar(
                    h[:], h[:], hq_col[:, ic:ic + 1], MAGIC, ALU.mult,
                    ALU.add), wait=[el, e_hq])
                V(lambda e, h=hs: e.tensor_scalar(
                    h[:], h[:], MAGIC, None, ALU.subtract))
                e_hqs = V(lambda e, h=hs, ic=ic, hq_sb=hq_sb: e.tensor_scalar(
                    hq_sb[:, ic * TCF:(ic + 1) * TCF], h[:],
                    -QMAX - 1.0, QMAX, ALU.max, ALU.min),
                    wait=lu(hq_sb) if ic == 0 else ())
                setlu(hs, e_hqs)
            hqs_ev[tch] = e_hqs

        q_stage(0)
        for tch in range(NCF):
            if stop in ("downq", "downs", "downm", "downm1", "downm2",
                        "downo", "downmA", "downmB") and tch == 1:
                return
            hq_sb = hq_sb2[tch % 2]
            e_hqs = hqs_ev[tch]
            if tch + 1 < NCF:
                q_stage(tch + 1)
            exs = []
            for sub in range(4):
                tc = tch * 4 + sub
                exs.append(dma_in(xt[sub][:],
                                  x_src[tc * 128:(tc + 1) * 128, :],
                                  "x%d" % sub, wait=lu(xt[sub]),
                                  eng="scalar"))
            nhc_run = NHC
            if stop in ("downq", "downs"):
                nhc_run = 0
            elif stop in ("downm1", "downmA", "downmB"):
                nhc_run = 1
            elif stop == "downm2":
                nhc_run = 2
            for hc in range(nhc_run):
                pd = [pA, pM][hc % 2]
                e_m = None
                for ic in range(NIC):
                    wv = []
                    if ic == 0:
                        wv = [e_hqs, e_wd] + lu(pd)
                        if e_add_prev is not None and hc == 0:
                            wv.append(e_add_prev)
                    e_m = TE(lambda e, ic=ic, hc=hc, pd=pd, hq_sb=hq_sb:
                             e.matmul(
                        pd[:, 0:512],
                        wdq[:, ic * H + hc * 128:ic * H + (hc + 1) * 128],
                        hq_sb[:, ic * TCF:(ic + 1) * TCF],
                        start=(ic == 0), stop=(ic == NIC - 1)), wait=wv)
                if stop == "downmA":
                    e_add_prev = e_m
                    continue
                setlu(hq_sb, e_m)
                e_e = S(lambda e, hc=hc, pd=pd: e.activation(
                    wrkA[:], pd[:, 0:512], AF.Identity,
                    scale=dn_col[:, hc:hc + 1]),
                    wait=[e_m, e_dn] + lu(wrkA))
                setlu(pd, e_e)
                if stop == "downmB":
                    setlu(wrkA, e_e)
                    e_add_prev = e_e
                    continue
                e_tl = None
                for sub in range(4):
                    pb = pT1 if sub % 2 == 0 else pT2
                    sl = (sub // 2) * 128
                    key = ("dn", sub % 2)
                    wv = [e_e, exs[sub]]
                    if sub < 2 and key in bank_ev:
                        wv.append(bank_ev[key])
                    e_tl = TE(lambda e, pb=pb, sl=sl, sub=sub: e.transpose(
                        pb[:, sl:sl + 128],
                        wrkA[:, sub * 128:(sub + 1) * 128], ident[:]),
                        wait=wv)
                e_add = None
                for sub in range(4):
                    pb = pT1 if sub % 2 == 0 else pT2
                    sl = (sub // 2) * 128
                    e_add = V(lambda e, pb=pb, sl=sl, sub=sub, hc=hc:
                              e.tensor_tensor(
                        xt[sub][:, hc * 128:(hc + 1) * 128],
                        xt[sub][:, hc * 128:(hc + 1) * 128],
                        pb[:, sl:sl + 128], ALU.add),
                        wait=[e_tl] if sub == 0 else ())
                    bank_ev[("dn", sub % 2)] = e_add
                setlu(wrkA, e_tl)
                e_add_prev = e_add
            if stop == "downq":
                continue
            if stop == "downs" and e_add_prev is None:
                e_add_prev = (CV, CV.v)
            for sub in range(4):
                tc = tch * 4 + sub
                e_o = dma_out(dst[tc * 128:(tc + 1) * 128, :], xt[sub][:],
                              "sx%d" % sub, wait=[e_add_prev], eng="scalar")
                setlu(xt[sub], e_o)
                if not final and not str(stop).startswith("downm"):
                    P.wait_only("scalar", [e_add_prev])
                    e_n = ln_stats_tile(xt[sub], tc, rstdB, nmrB, False, None)
                    bt = xqt[sub % 2]
                    e_a = S(lambda e, sub=sub, tc=tc, b=bt: e.activation(
                        b[:], xt[sub][:], AF.Identity,
                        bias=nmrB[:, tc:tc + 1], scale=rstdB[:, tc:tc + 1]),
                        wait=[e_n] + lu(bt))
                    addlu(xt[sub], e_a)
                    bank = (sub % 2) * 1024
                    e_t = None
                    for hc in range(NHC):
                        wv = [e_a]
                        if hc == 0 and ("x", bank) in bank_ev:
                            wv.append(bank_ev[("x", bank)])
                        e_t = TE(lambda e, b=bt, hc=hc, sl=bank + hc * 128:
                                 e.transpose(
                            pBF[:, sl:sl + 128],
                            b[:, hc * 128:(hc + 1) * 128], identb[:]),
                            wait=wv)
                    e_c = None
                    for hc in range(NHC):
                        sg = stg_b[hc % 2]
                        e_c = V(lambda e, sl=bank + hc * 128, sg=sg:
                                e.tensor_copy(
                            sg[:, 0:128], pBF[:, sl:sl + 128]),
                            wait=([e_t] if hc == 0 else []) + lu(sg))
                        es = dma_out(
                            s_xln2[hc, tc],
                            sg[:, 0:128], "xl%d" % (hc % 2), wait=[e_c],
                            eng="scalar")
                        setlu(sg, es)
                    bank_ev[("x", bank)] = e_c
                    setlu(bt, e_c)

    e_hq, e_dn = ffn_front(x_in, 0, 1, (rstdA, nmrA), wgq, wuq, xq,
                           xqt1, pBF1, wrkB1, wrkC1, wrkD1)
    barrier()
    if stop in ("front", "stats", "gmax", "quant", "gateup"):
        P.wait_only("sync", [(c, c.v) for c in ALLC])
        wstk1.close()
        ffn_s.close()
        root.close()
        return nc, P
    wstk1.close()
    dstk = contextlib.ExitStack()
    ffn_down(x_in, o_out if stop == "ffn1" else s_x1, e_hq, e_dn,
             final=(stop == "ffn1"), stk=dstk, xqt=xqt1, stg_b=stg1,
             pBF=pBF1, pT2=pT21, xt=xtf1)
    barrier()
    dstk.close()
    ffn_s.close()
    if stop in ("ffn1", "downq", "downs", "downm", "downm1", "downm2", "downo", "downmA", "downmB"):
        P.wait_only("sync", [(c, c.v) for c in ALLC])
        root.close()
        return nc, P

    mam = contextlib.ExitStack()
    selBC = mk(mam, "selBC", [16, NST * 128])
    bcC = mk(mam, "bcC", [16, TCM])
    dA = [mk(mam, f"dA{i}", [128, WSC]) for i in range(2)]
    dBu2 = [mk(mam, f"dBu{i}", [128, WSC]) for i in range(2)]
    stt = mk(mam, "stt", [128, WSC])
    Ball = mk(mam, "Ball", [128, WSC])
    Call = mk(mam, "Call", [128, WSC])
    xp = [mk(mam, f"xp{i}", [128, TCM + 3]) for i in range(2)]
    tails = mk(mam, "tails", [128, NDC * 3])
    u_t4 = [mk(mam, f"u{i}", [128, TCM]) for i in range(NDC)]
    sz_t = [mk(mam, f"sz{i}", [128, TCM]) for i in range(NDC)]
    ubf = [mk(mam, f"ubf{i}", [128, TCM], BF16) for i in range(2)]
    dl_t = [mk(mam, f"dl{i}", [128, TCM]) for i in range(2)]
    du_t = [mk(mam, f"du{i}", [128, TCM]) for i in range(2)]
    dltb = mk(mam, "dltb", [RNK, TCM], BF16)
    bc_sb = mk(mam, "bc_sb", [32, TCM])
    carry = [mk(mam, f"cr{i}", [128, NST]) for i in range(NDC)]
    ytmp = mk(mam, "ytmp", [128, TCM])
    ybf = [mk(mam, f"ybf{i}", [128, TCM], BF16) for i in range(2)]
    win_g2 = [mk(mam, f"wing{i}", [128, H], BF16) for i in range(2)]
    xl_c = [mk(mam, f"xl{i}", [128, TCM], BF16) for i in range(NHC)]
    wo_c = [mk(mam, f"wo{i}", [128, H], BF16) for i in range(2)]
    wk1p = [mk(mam, f"wk1_{i}", [128, TCM]) for i in range(2)]
    wk2m1 = mk(mam, "wk2m", [128, TCM])
    wk2mp = [wk2m1, wk2m1]
    pW = mkp(mam, "pW", [128, 2048], F32)

    G(lambda g: g.memset(selBC[:], 0.0))
    G(lambda g: g.affine_select(
        out=selBC[:], in_=selBC[:],
        compare_op=ALU.not_equal, fill=1.0, base=0,
        pattern=[[-1, NST], [0, 128]], channel_multiplier=1))
    G(lambda g: g.memset(dA[0][:], 0.0))
    G(lambda g: g.memset(dA[1][:], 0.0))
    G(lambda g: g.memset(Ball[:], 0.0))
    G(lambda g: g.memset(Call[:], 0.0))
    for dc in range(NDC):
        G(lambda g, dc=dc: g.memset(carry[dc][:], 0.0))
    G(lambda g: g.memset(tails[:], 0.0))
    e_ms = G(lambda g: g.memset(ytmp[:], 0.0))

    def seg(t, n):
        return t[:, n * GAP + 1:(n + 1) * GAP]

    def gapped3(t):
        return t[:].rearrange("p (n g) -> p n g", n=NST)[:, :, 1:]

    def bounds3(t):
        return t[:].rearrange("p (n g) -> p n g", n=NST)[:, :, 0:1]

    e_x2w = []
    sz_pe = {}
    treev = {}
    for c in range(NCM):
        e_w = {}
        for hc in range(NHC):
            dma_in(xl_c[hc][:, 0:128], s_xln2[hc, 2 * c],
                   "mx%d" % (hc % 4), wait=lu(xl_c[hc]))
            e_w[hc] = dma_in(xl_c[hc][:, 128:256], s_xln2[hc, 2 * c + 1],
                             "mx%d" % (hc % 4))
        pps = [pA, pB]
        e_xp = {}
        e_u = {}
        e_xdbl = None
        cv_ev = {}

        def u_stage(dc):
            wk1 = wk1p[dc % 2]
            e_uu = S(lambda e, dc=dc, wk1=wk1: e.activation(
                u_t4[dc][:], wk1[:], AF.Silu,
                bias=convb_t[:, dc:dc + 1]),
                wait=[cv_ev[dc]] + lu(u_t4[dc]))
            setlu(wk1, e_uu)
            e_u[dc] = e_uu
            e_ub = S(lambda e, dc=dc: e.activation(
                ubf[dc % 2][:], u_t4[dc][:], AF.Identity),
                wait=lu(ubf[dc % 2]))
            wv2 = [e_ub] + (lu(pM) if dc == 0 else [])
            e_x = TE(lambda e, dc=dc: e.matmul(
                pM[0:96, 0:TCM], wx_b[:, dc * 96:(dc + 1) * 96],
                ubf[dc % 2][:], start=(dc == 0), stop=(dc == NDC - 1)),
                wait=wv2)
            setlu(ubf[dc % 2], e_x)
            return e_x

        for g2 in range(DXZ // 128):
            pp = pps[g2 % 2]
            wt = win_g2[g2 % 2]
            e_wg = dma_in(wt[:], s_winbf2[g2 * 128:(g2 + 1) * 128, :],
                          "wg%d" % (g2 % 2), wait=lu(wt))
            e_m = None
            for hc in range(NHC):
                wv = [e_wg] + ([e_w[hc]] if g2 == 0 else []) + \
                    (lu(pp) if hc == 0 else [])
                e_m = TE(lambda e, hc=hc, pp=pp, wt=wt: e.matmul(
                    pp[:, 0:TCM], wt[:, hc * 128:(hc + 1) * 128],
                    xl_c[hc][:], start=(hc == 0), stop=(hc == NHC - 1)),
                    wait=wv if (hc == 0 or g2 == 0) else [])
                setlu(xl_c[hc], e_m)
            setlu(wt, e_m)
            if g2 < NDC:
                dc = g2
                xpt = xp[dc % 2]
                e_mg = G(lambda e, dc=dc, x_=xpt: e.tensor_copy(
                    x_[:, 0:3], tails[:, dc * 3:dc * 3 + 3]),
                    wait=([e_ms] if c == 0 else []) + lu(xpt))
                e_c = S(lambda e, x_=xpt, pp=pp: e.activation(
                    x_[:, 3:TCM + 3], pp[:, 0:TCM], AF.Identity),
                    wait=[e_m, e_mg])
                e_xp[dc] = e_c
                def cwb(col):
                    return convw_t[:, col:col + 1].rearrange(
                        "p (o t) -> p o t", o=1).broadcast_to([128, 1, TCM])

                def v3(t):
                    return t.rearrange("p (o t) -> p o t", o=1)

                wk1 = wk1p[dc % 2]
                wk2m = wk2mp[dc % 2]
                e_cv = G(lambda e, dc=dc, x_=xpt, wk1=wk1: e.tensor_tensor(
                    v3(wk1[:]), v3(x_[:, 0:TCM]), cwb(dc * 4), ALU.mult),
                    wait=[e_c] + lu(wk1))
                for j in range(1, 4):
                    G(lambda e, dc=dc, j=j, x_=xpt, wk2m=wk2m: e.tensor_tensor(
                        v3(wk2m[:]), v3(x_[:, j:j + TCM]), cwb(dc * 4 + j),
                        ALU.mult), wait=lu(wk2m))
                    e_cv = G(lambda e, wk1=wk1, wk2m=wk2m: e.tensor_tensor(
                        wk1[:], wk1[:], wk2m[:], ALU.add))
                    setlu(wk2m, e_cv)
                e_tl = G(lambda e, dc=dc, x_=xpt: e.tensor_copy(
                    tails[:, dc * 3:dc * 3 + 3], x_[:, TCM:TCM + 3]),
                    wait=[e_c])
                setlu(xpt, e_tl)
                cv_ev[dc] = e_cv
                e_c = e_xp[dc]
            else:
                dc = g2 - NDC
                e_c = S(lambda e, dc=dc, pp=pp: e.activation(
                    sz_t[dc][:], pp[:, 0:TCM], AF.Silu),
                    wait=[e_m] + lu(sz_t[dc]))
                sz_pe[dc] = e_c
            setlu(pp, e_c)
            if 1 <= g2 <= NDC:
                e_xdbl = u_stage(g2 - 1)
        e_dlt = V(lambda e: e.tensor_copy(dltb[:], pM[0:RNK, 0:TCM]),
                  wait=[e_xdbl] + lu(dltb))
        e_bcv = V(lambda e: e.tensor_copy(bc_sb[:], pM[64:96, 0:TCM]),
                  wait=lu(bc_sb))
        setlu(pM, e_bcv)
        e_bcc = dma_in(bcC[:], bc_sb[16:32, :], "bcc", wait=[e_bcv] + lu(bcC),
                       eng="scalar")
        e_bev = None
        e_cev = None
        e_tb = None
        e_tc = None
        for n in range(NST):
            e_tb = TE(lambda e, n=n: e.matmul(
                pM[:, 0:TCM], selBC[0:16, n * 128:(n + 1) * 128],
                bc_sb[0:16, :], start=True, stop=True),
                wait=([e_bcv] if n == 0 else [e_bev]) + lu(pM))
            e_bev = S(lambda e, n=n: e.activation(
                seg(Ball, n), pM[:, 0:TCM], AF.Identity),
                wait=[e_tb] + (lu(Ball) if n == 0 else []))
            setlu(pM, e_bev)
            e_tc = TE(lambda e, n=n: e.matmul(
                pB[:, 0:TCM], selBC[0:16, n * 128:(n + 1) * 128],
                bcC[:], start=True, stop=True),
                wait=([e_bcc] + lu(pB) if n == 0 else [e_cev]))
            e_cev = V(lambda e, n=n: e.tensor_copy(
                seg(Call, n), pB[:, 0:TCM]),
                wait=[e_tc] + (lu(Call) if n == 0 else []))
        setlu(pB, e_cev)
        setlu(bc_sb, e_tb)
        setlu(bcC, e_tc)
        e_ball = e_bev
        e_call = e_cev
        ev = {}

        def stage_prep_sp(dc):
            pp = pps[dc % 2]
            dl = dl_t[dc % 2]
            du = du_t[dc % 2]
            e_m = TE(lambda e, dc=dc, pp=pp: e.matmul(
                pp[:, 0:TCM], wdt_b[:, dc * 128:(dc + 1) * 128], dltb[:],
                start=True, stop=True), wait=[e_dlt] + lu(pp))
            setlu(dltb, e_m)
            S(lambda e, dc=dc, pp=pp, du=du: e.activation(
                du[:], pp[:, 0:TCM], AF.Abs, bias=bdt_t[:, dc:dc + 1]),
                wait=[e_m] + lu(du))
            S(lambda e, du=du: e.activation(du[:], du[:], AF.Exp, scale=-1.0))
            e_s1 = S(lambda e, du=du: e.activation(du[:], du[:], AF.Ln,
                                                   bias=1.0))
            e_s2 = S(lambda e, dc=dc, pp=pp, dl=dl: e.activation(
                dl[:], pp[:, 0:TCM], AF.Relu, bias=bdt_t[:, dc:dc + 1]),
                wait=lu(dl))
            setlu(pp, e_s2)
            ev[("sp", dc)] = e_s2

        def stage_prep_v(dc):
            dl = dl_t[dc % 2]
            du = du_t[dc % 2]
            e_d = V(lambda e, dl=dl, du=du: e.tensor_tensor(
                dl[:], dl[:], du[:], ALU.add), wait=[ev[("sp", dc)]])
            e_du = V(lambda e, dl=dl, du=du, dc=dc: e.tensor_tensor(
                du[:], dl[:], u_t4[dc][:], ALU.mult),
                wait=[e_u[dc]])
            addlu(u_t4[dc], e_du)
            da = dA[dc % 2]
            e_e = None
            for n in range(NST):
                wv = [e_d] + (lu(da) if n == 0 else [])
                e_e = S(lambda e, dc=dc, n=n, dl=dl, da=da: e.activation(
                    seg(da, n), dl[:], AF.Exp,
                    scale=acol[:, dc * NST + n:dc * NST + n + 1]), wait=wv)
            setlu(dl, e_e)
            ev[("exp", dc)] = e_e
            ev[("du", dc)] = e_du

        def g3(t, a, b):
            return t[:, a * GAP:b * GAP].rearrange(
                "p (n g) -> p n g", n=b - a)[:, :, 1:]

        def stage_dbu_g(dc):
            du = du_t[dc % 2]
            dbu = dBu2[dc % 2]
            tv = treev.get(dc % 2, {})

            def du3(nn):
                return du[:].rearrange("p (o t) -> p o t",
                                       o=1).broadcast_to([128, nn, TCM])
            G(lambda e, dbu=dbu, du=du: e.tensor_tensor(
                g3(dbu, 8, 12), du3(4), g3(Ball, 8, 12), ALU.mult),
                wait=[ev[("du", dc)], e_ball])
            G(lambda e, dbu=dbu, du=du: e.tensor_tensor(
                g3(dbu, 12, 16), du3(4), g3(Ball, 12, 16), ALU.mult))
            G(lambda e, dbu=dbu, du=du: e.tensor_tensor(
                g3(dbu, 4, 8), du3(4), g3(Ball, 4, 8), ALU.mult),
                wait=[tv["lvl2"]] if "lvl2" in tv else [])
            e_lo = G(lambda e, dbu=dbu, du=du: e.tensor_tensor(
                g3(dbu, 0, 4), du3(4), g3(Ball, 0, 4), ALU.mult),
                wait=[tv["yadd"]] if "yadd" in tv else [])
            e_inj = G(lambda e, dc=dc, dbu=dbu: e.tensor_copy(
                bounds3(dbu),
                carry[dc][:].rearrange("p (n o) -> p n o", o=1)),
                wait=lu(carry[dc]))
            setlu(du, e_lo)
            addlu(Ball, e_lo)
            ev[("dbuL", dc)] = e_lo
            ev[("inj", dc)] = e_inj

        def stage_scan_v(dc):
            da = dA[dc % 2]
            dbu = dBu2[dc % 2]
            e_scan = V(lambda e, da=da, dbu=dbu: e.tensor_tensor_scan(
                stt[:], da[:], dbu[:], 0.0, ALU.mult, ALU.add),
                wait=[ev[("exp", dc)], ev[("dbuL", dc)], ev[("inj", dc)]] +
                lu(stt))
            setlu(da, e_scan)
            e_cx = S(lambda e, dc=dc: e.activation(
                carry[dc][:],
                stt[:].rearrange("p (n g) -> p n g", n=NST)[:, :, GAP - 1],
                AF.Identity), wait=[e_scan])
            setlu(carry[dc], e_cx)
            V(lambda e, dbu=dbu: e.tensor_tensor(
                dbu[:, 0:8 * GAP], stt[:, 0:8 * GAP],
                Call[:, 0:8 * GAP], ALU.mult), wait=[e_call])
            e_ym = V(lambda e, dbu=dbu: e.tensor_tensor(
                dbu[:, 8 * GAP:16 * GAP], stt[:, 8 * GAP:16 * GAP],
                Call[:, 8 * GAP:16 * GAP], ALU.mult))
            setlu(stt, e_ym)
            addlu(stt, e_cx)
            addlu(Call, e_ym)
            ev[("ym", dc)] = e_ym

        def stage_tail_g(dc):
            dbu = dBu2[dc % 2]
            ev[("lvl1", dc)] = G(lambda e, dbu=dbu: e.tensor_tensor(
                dbu[:, 0:8 * GAP], dbu[:, 0:8 * GAP],
                dbu[:, 8 * GAP:16 * GAP], ALU.add),
                wait=[ev[("ym", dc)]])

        def stage_tail_v(dc):
            dbu = dBu2[dc % 2]
            tre = {}
            e_tr = V(lambda e, dbu=dbu: e.tensor_tensor(
                dbu[:, 0:4 * GAP], dbu[:, 0:4 * GAP],
                dbu[:, 4 * GAP:8 * GAP], ALU.add),
                wait=[ev[("lvl1", dc)]])
            tre["lvl2"] = e_tr
            e_y1 = V(lambda e, dc=dc, dbu=dbu: e.scalar_tensor_tensor(
                ytmp[:], u_t4[dc][:], dpar_t[:, dc:dc + 1],
                dbu[:, 1:GAP], ALU.mult, ALU.add), wait=[e_tr] + lu(ytmp))
            setlu(u_t4[dc], e_y1)
            for n in range(1, 4):
                e_y1 = V(lambda e, n=n, dbu=dbu: e.tensor_tensor(
                    ytmp[:], ytmp[:],
                    dbu[:, n * GAP + 1:(n + 1) * GAP], ALU.add))
            tre["yadd"] = e_y1
            treev[dc % 2] = tre
            e_y2 = V(lambda e, dc=dc: e.tensor_tensor(
                ytmp[:], ytmp[:], sz_t[dc][:], ALU.mult),
                wait=[sz_pe[dc]])
            setlu(sz_t[dc], e_y2)
            ev[("y2", dc)] = e_y2

        def stage_yb(dc):
            yb = ybf[dc % 2]
            e_yb = S(lambda e, yb=yb: e.activation(
                yb[:], ytmp[:], AF.Identity),
                wait=[ev[("y2", dc)]] + lu(yb))
            setlu(ytmp, e_yb)
            ev[("yb", dc)] = e_yb

        def stage_wout(dc):
            yb = ybf[dc % 2]
            wo = wo_c[dc % 2]
            e_wo = dma_in(wo[:], s_woutbf[dc * 128:(dc + 1) * 128, :],
                          "wo%d" % (dc % 2), wait=lu(wo))
            e_wm = None
            for hc in range(NHC):
                wv = [ev[("yb", dc)], e_wo]
                if dc == 0 and hc == 0:
                    wv = wv + lu(pW)
                e_wm = TE(lambda e, hc=hc, wo=wo, yb=yb, dc=dc: e.matmul(
                    pW[:, hc * TCM:(hc + 1) * TCM],
                    wo[:, hc * 128:(hc + 1) * 128], yb[:],
                    start=(dc == 0 and hc % 2 == 0),
                    stop=(dc == NDC - 1 and hc % 2 == 1)),
                    wait=wv if hc == 0 else [])
            setlu(wo, e_wm)
            setlu(yb, e_wm)
            ev[("wm", dc)] = e_wm

        stage_prep_sp(0)
        stage_prep_sp(1)
        stage_prep_v(0)
        stage_dbu_g(0)
        for dc in range(NDC):
            if dc >= 1:
                stage_tail_g(dc - 1)
            if dc + 1 < NDC:
                stage_prep_v(dc + 1)
            stage_scan_v(dc)
            if dc >= 1:
                stage_tail_v(dc - 1)
            if dc + 2 < NDC:
                stage_prep_sp(dc + 2)
            if dc >= 1:
                stage_yb(dc - 1)
            if dc + 1 < NDC:
                stage_dbu_g(dc + 1)
            if dc >= 1:
                stage_wout(dc - 1)
        stage_tail_g(NDC - 1)
        stage_tail_v(NDC - 1)
        stage_yb(NDC - 1)
        stage_wout(NDC - 1)
        e_wm = ev[("wm", NDC - 1)]
        exs = []
        for sub in range(2):
            tc = c * 2 + sub
            exs.append(dma_in(xt01[sub][:],
                              s_x1[tc * 128:(tc + 1) * 128, :],
                              "x%d" % sub, wait=lu(xt01[sub]), eng="scalar"))
        e_add = None
        e_padd = None
        for hc in range(NHC):
            e_e = S(lambda e, hc=hc: e.activation(
                wrkA[:, 0:TCM], pW[:, hc * TCM:(hc + 1) * TCM], AF.Identity),
                wait=[e_wm] + lu(wrkA))
            e_t = None
            for sub in range(2):
                e_t = TE(lambda e, sub=sub, hc=hc: e.transpose(
                    pT1[:, sub * 128:(sub + 1) * 128],
                    wrkA[:, sub * 128:(sub + 1) * 128], ident[:]),
                    wait=[e_e, exs[sub]] +
                    ([e_padd] if e_padd is not None and sub == 0 else []) +
                    (lu(pT1) if hc == 0 and sub == 0 else []))
            for sub in range(2):
                e_add = V(lambda e, sub=sub, hc=hc: e.tensor_tensor(
                    xt01[sub][:, hc * 128:(hc + 1) * 128],
                    xt01[sub][:, hc * 128:(hc + 1) * 128],
                    pT1[:, sub * 128:(sub + 1) * 128], ALU.add),
                    wait=[e_t] if sub == 0 else ())
            e_padd = e_add
            setlu(wrkA, e_t)
            setlu(pT1, e_add)
        setlu(pW, e_e)
        for sub in range(2):
            tc = c * 2 + sub
            dst2 = o_out if stop == "mamba" else s_x2
            e_o = dma_out(dst2[tc * 128:(tc + 1) * 128, :], xt01[sub][:],
                          "sx%d" % sub, wait=[e_add], eng="scalar")
            e_x2w.append(e_o)
            setlu(xt01[sub], e_o)
            if stop != "mamba":
                P.wait_only("scalar", [e_add])
                e_n = ln_stats_tile(xt01[sub], tc, rstdC, nmrC, True, lmax)
                addlu(xt01[sub], e_n)

    barrier()
    mam.close()
    if stop == "mamba":
        P.wait_only("sync", [(c, c.v) for c in ALLC])
        root.close()
        return nc, P

    ffn2 = contextlib.ExitStack()
    xq2 = mk(ffn2, "xq2", [128, NHC * T], BF16)
    xqt2 = [mk(ffn2, f"xqt2_{i}", [128, H], BF16) for i in range(2)]
    stg2 = [mk(ffn2, f"stg2_{i}", [128, TCF], BF16) for i in range(2)]
    xtf2 = xt01 + [mk(ffn2, f"xtf2_{i}", [128, H]) for i in range(2)]
    wrkB2 = mk(ffn2, "wrkB2", [128, TCF])
    wrkC2 = mk(ffn2, "wrkC2", [128, TCF])
    wrkD2 = mk(ffn2, "wrkD2", [128, TCF])
    pBF2 = mkp(ffn2, "pBF2", [128, 2048], BF16)
    pT22 = mkp(ffn2, "pT22", [128, 512], F32)
    wstk2 = contextlib.ExitStack()
    wgq2, wuq2 = load_wgu(wstk2, "2")
    e_hq2, e_dn2 = ffn_front(s_x2, 2, 3, (rstdC, nmrC), wgq2, wuq2,
                             xq2, xqt2, pBF2, wrkB2, wrkC2,
                             wrkD2)
    barrier()
    wstk2.close()
    dstk2 = contextlib.ExitStack()
    ffn_down(s_x2, o_out, e_hq2, e_dn2, final=True, stk=dstk2, xqt=xqt2,
             stg_b=stg2, pBF=pBF2, pT2=pT22, xt=xtf2)
    barrier()
    dstk2.close()
    ffn2.close()
    P.wait_only("sync", [(c, c.v) for c in ALLC])
    root.close()
    return nc, P


def _emit(nc, P):
    with nc.Block() as block:
        for eng in Prog.ENGINES:
            def section(e, eng=eng):
                for fn in P.streams[eng]:
                    fn(e)
            getattr(block, eng)(section)
    return nc


_NC_CACHE = {}


def _get_nc():
    key = _STOP
    if key not in _NC_CACHE:
        nc, P = build_nc(stop=_STOP)
        _NC_CACHE[key] = _emit(nc, P)
    return _NC_CACHE[key]


def kernel(L0_attn_x, L0_attn_W_in, L0_attn_conv_w, L0_attn_conv_b,
           L0_attn_W_x, L0_attn_W_dt, L0_attn_b_dt, L0_attn_A_log,
           L0_attn_D_param, L0_attn_W_out, L0_ffn_clip_x, L0_ffn_clip_down,
           L0_ffn_w_gate, L0_ffn_w_up, L0_ffn_w_down):
    def f32c(a):
        return np.ascontiguousarray(np.asarray(a, dtype=np.float32))

    def colmaj(v):
        return f32c(np.asarray(v).reshape(NDC, 128).T)

    x = f32c(L0_attn_x)
    shared = dict(
        winT2=f32c(np.asarray(L0_attn_W_in).reshape(32, 128, 8, 128)
                   .transpose(0, 3, 2, 1).reshape(DXZ, H)),
        wx_r=f32c(np.asarray(L0_attn_W_x).T.reshape(NDC, 128, 96)
                  .transpose(1, 0, 2).reshape(128, NDC * 96)),
        wdtT=f32c(np.asarray(L0_attn_W_dt).T),
        woutT=f32c(np.asarray(L0_attn_W_out).T),
        wg_n=f32c(L0_ffn_w_gate),
        wgT=f32c(np.asarray(L0_ffn_w_gate).T),
        wu_n=f32c(L0_ffn_w_up),
        wuT=f32c(np.asarray(L0_ffn_w_up).T),
        wd_n=f32c(L0_ffn_w_down),
        wdT=f32c(np.asarray(L0_ffn_w_down).T),
        convw=f32c(np.asarray(L0_attn_conv_w)[:, 0, :]
                   .reshape(NDC, 128, 4).transpose(1, 0, 2)
                   .reshape(128, NDC * 4)),
        convb=colmaj(L0_attn_conv_b),
        bdt=colmaj(L0_attn_b_dt),
        alog=f32c(np.asarray(L0_attn_A_log).reshape(NDC, 128, NST)
                  .transpose(1, 0, 2).reshape(128, NDC * NST)),
        dpar=colmaj(L0_attn_D_param),
        clip2=f32c(np.stack([np.asarray(L0_ffn_clip_x).ravel()[0],
                             np.asarray(L0_ffn_clip_down).ravel()[0]])
                   .reshape(1, 2)),
    )
    in_maps = [dict(shared, x=f32c(x[c])) for c in range(8)]
    nc = _get_nc()
    res = run_bass_kernel_spmd(nc, in_maps, core_ids=list(range(8)))
    return np.stack([np.asarray(res.results[c]["o"], dtype=np.float32)
                     for c in range(8)], axis=0)


# version 20 (final)
# speedup vs baseline: 1.0123x; 1.0123x over previous
import contextlib
import numpy as np
import concourse.bass as bass
import concourse.mybir as mybir
from concourse.bass_utils import run_bass_kernel_spmd

F32 = mybir.dt.float32
BF16 = mybir.dt.bfloat16
AF = mybir.ActivationFunctionType
ALU = mybir.AluOpType
AX = mybir.AxisListType.X

T = 2048
H = 1024
DXZ = 4096
D = 2048
NST = 16
RNK = 64
II = 2816
NHC = H // 128
NDC = D // 128
NIC = II // 128
NTC = T // 128
TCF = 512
NCF = T // TCF
TCM = 256
NCM = T // TCM
GAP = TCM + 1
WSC = NST * GAP
MAGIC = 12582912.0
EPS = 1e-5
QMAX = 127.0
RG = [[0, 1, 2, 3, 4, 5, 6, 7]]

_STOP = None


class Ctr:
    def __init__(self, sem):
        self.sem = sem
        self.v = 0


class Prog:
    ENGINES = ("sync", "gpsimd", "vector", "scalar", "tensor")

    def __init__(self):
        self.streams = {e: [] for e in self.ENGINES}

    def add(self, eng, fn, wait=(), inc=None, inc_k=1):
        waits = [(c.sem, v) for (c, v) in wait if v > 0]
        if inc is not None:
            inc.v += inc_k
        sem = inc.sem if inc is not None else None

        def run(e, fn=fn, waits=waits, sem=sem, inc_k=inc_k):
            for s, v in waits:
                e.wait_ge(s, v)
            ins = fn(e)
            if sem is not None:
                ins.then_inc(sem, inc_k)
        self.streams[eng].append(run)

    def wait_only(self, eng, wait):
        waits = [(c.sem, v) for (c, v) in wait if v > 0]
        if not waits:
            return

        def run(e, waits=waits):
            for s, v in waits:
                e.wait_ge(s, v)
        self.streams[eng].append(run)


def build_nc(stop=None):
    nc = bass.Bass("TRN2", target_bir_lowering=False)

    def ein(name, shape, dtype=F32):
        return nc.dram_tensor(name, shape, dtype, kind="ExternalInput").ap()

    x_in = ein("x", [T, H])
    winT2 = ein("winT2", [DXZ, H])
    wx_r = ein("wx_r", [128, NDC * 96])
    wdtT = ein("wdtT", [RNK, D])
    woutT = ein("woutT", [D, H])
    wg_n = ein("wg_n", [II, H])
    wgT = ein("wgT", [H, II])
    wu_n = ein("wu_n", [II, H])
    wuT = ein("wuT", [H, II])
    wd_n = ein("wd_n", [H, II])
    wdT = ein("wdT", [II, H])
    convw = ein("convw", [128, NDC * 4])
    convb = ein("convb", [128, NDC])
    bdt = ein("bdt", [128, NDC])
    alog = ein("alog", [128, NDC * NST])
    dpar = ein("dpar", [128, NDC])
    clip2 = ein("clip2", [1, 2])

    o_out = nc.dram_tensor("o", [T, H], F32, kind="ExternalOutput").ap()

    s_winbf2 = nc.dram_tensor("s_winbf2", [DXZ, H], BF16, kind="Internal").ap()
    s_woutbf = nc.dram_tensor("s_woutbf", [D, H], BF16, kind="Internal").ap()
    s_wgq = nc.dram_tensor("s_wgq", [H, II], BF16, kind="Internal").ap()
    s_wuq = nc.dram_tensor("s_wuq", [H, II], BF16, kind="Internal").ap()
    s_wdq = nc.dram_tensor("s_wdq", [II, H], BF16, kind="Internal").ap()
    s_h = nc.dram_tensor("s_h", [NIC, NCF, 128, TCF], F32,
                         kind="Internal").ap()
    s_x1 = nc.dram_tensor("s_x1", [T, H], F32, kind="Internal").ap()
    s_x2 = nc.dram_tensor("s_x2", [T, H], F32, kind="Internal").ap()
    s_xln2 = nc.dram_tensor("s_xln2", [NHC, NTC, 128, 128], BF16,
                            kind="Internal").ap()
    gmi = [nc.dram_tensor(f"gmi{i}", [1, 1], F32, kind="Internal").ap()
           for i in range(4)]
    gmo = [nc.dram_tensor(f"gmo{i}", [1, 1], F32, kind="Internal").ap()
           for i in range(4)]

    P = Prog()
    root = contextlib.ExitStack()

    def mk(stk, name, shape, dtype=F32):
        return stk.enter_context(nc.sbuf_tensor(name, shape, dtype))

    def mkp(stk, name, shape, dtype=F32):
        return stk.enter_context(nc.psum_tensor(name, shape, dtype))

    ident = mk(root, "ident", [128, 128], F32)
    identb = mk(root, "identb", [128, 128], BF16)
    ones1 = mk(root, "ones1", [1, 128], F32)
    convw_t = mk(root, "convw_t", [128, NDC * 4])
    convb_t = mk(root, "convb_t", [128, NDC])
    bdt_t = mk(root, "bdt_t", [128, NDC])
    acol = mk(root, "acol", [128, NDC * NST])
    dpar_t = mk(root, "dpar_t", [128, NDC])
    wx_b = mk(root, "wx_b", [128, NDC * 96], BF16)
    wdt_b = mk(root, "wdt_b", [RNK, D], BF16)
    clip_t = mk(root, "clip_t", [128, 2])
    rstdA = mk(root, "rstdA", [128, NTC])
    nmrA = mk(root, "nmrA", [128, NTC])
    rstdB = mk(root, "rstdB", [128, NTC])
    nmrB = mk(root, "nmrB", [128, NTC])
    rstdC = mk(root, "rstdC", [128, NTC])
    nmrC = mk(root, "nmrC", [128, NTC])
    sumS = mk(root, "sumS", [128, NTC])
    sqS = mk(root, "sqS", [128, NTC])
    scS = mk(root, "scS", [128, NTC])
    biS = mk(root, "biS", [128, NTC])
    lmax = mk(root, "lmax", [128, NIC])
    col1 = mk(root, "col1", [128, 1])
    col2 = mk(root, "col2", [128, 1])
    grow = mk(root, "grow", [1, 128])
    gsb = mk(root, "gsb", [1, 1])
    sx_rep = mk(root, "sx_rep", [128, 1])
    isx_rep = mk(root, "isx_rep", [128, 1])
    sh_rep = mk(root, "sh_rep", [128, 1])
    ish_rep = mk(root, "ish_rep", [128, 1])
    sg_col = mk(root, "sg_col", [128, NIC])
    su_col = mk(root, "su_col", [128, NIC])
    sd_col = mk(root, "sd_col", [128, NHC])
    sxg_col = mk(root, "sxg_col", [128, NIC])
    sxu_col = mk(root, "sxu_col", [128, NIC])
    hq_col = mk(root, "hq_col", [128, NIC])
    dn_col = mk(root, "dn_col", [128, NHC])
    xt01 = [mk(root, f"xt{i}", [128, H]) for i in range(2)]
    qa = mk(root, "qa", [128, H])
    wrkA = mk(root, "wrkA", [128, TCF])

    pA = mkp(root, "pA", [128, 512], F32)
    pB = mkp(root, "pB", [128, 512], F32)
    pM = mkp(root, "pM", [128, 512], F32)
    pT1 = mkp(root, "pT1", [128, 512], F32)

    CV = Ctr(root.enter_context(nc.semaphore("CV")))
    CS = Ctr(root.enter_context(nc.semaphore("CS")))
    CT = Ctr(root.enter_context(nc.semaphore("CT")))
    CG = Ctr(root.enter_context(nc.semaphore("CG")))
    CC = Ctr(root.enter_context(nc.semaphore("CC")))
    ALLC = [CV, CS, CT, CG, CC]
    _dsem = {}

    def DS(name):
        if name not in _dsem:
            _dsem[name] = Ctr(root.enter_context(nc.semaphore("D_" + name)))
            ALLC.append(_dsem[name])
        return _dsem[name]

    last_use = {}

    def lu(buf):
        return list(last_use.get(id(buf), []))

    def setlu(buf, ev):
        last_use[id(buf)] = [ev]

    def addlu(buf, ev):
        last_use.setdefault(id(buf), []).append(ev)

    def dma_in(dst, src, stream="ld", wait=(), eng="sync"):
        c = DS("i" + stream)
        P.add(eng, lambda e, d=dst, s=src: e.dma_start(out=d, in_=s),
              wait=tuple(wait) + ((c, c.v),), inc=c, inc_k=16)
        return (c, c.v)

    def dma_out(dst, src, stream="st", wait=(), eng="gpsimd"):
        c = DS("o" + stream)
        P.add(eng, lambda e, d=dst, s=src: e.dma_start(out=d, in_=s),
              wait=tuple(wait) + ((c, c.v),), inc=c, inc_k=16)
        return (c, c.v)

    def barrier():
        vals = [(c, c.v) for c in ALLC]
        for e in Prog.ENGINES:
            P.wait_only(e, vals)

    def V(fn, wait=()):
        P.add("vector", fn, wait=tuple(wait) + ((CV, CV.v),), inc=CV)
        return (CV, CV.v)

    def S(fn, wait=()):
        P.add("scalar", fn, wait=tuple(wait) + ((CS, CS.v),), inc=CS)
        return (CS, CS.v)

    def TE(fn, wait=()):
        P.add("tensor", fn, wait=wait, inc=CT)
        return (CT, CT.v)

    def G(fn, wait=()):
        P.add("gpsimd", fn, wait=tuple(wait) + ((CG, CG.v),), inc=CG)
        return (CG, CG.v)

    if stop == "smoke":
        smk = mk(root, "smk", [128, H])
        sm_sem = Ctr(root.enter_context(nc.semaphore("SMK")))
        for tc in range(NTC):
            P.add("sync", lambda e, tc=tc: e.dma_start(
                out=smk[:], in_=x_in[tc * 128:(tc + 1) * 128, :]),
                wait=[(sm_sem, 32 * tc)], inc=sm_sem, inc_k=16)
            P.add("sync", lambda e, tc=tc: e.dma_start(
                out=o_out[tc * 128:(tc + 1) * 128, :], in_=smk[:]),
                wait=[(sm_sem, 32 * tc + 16)], inc=sm_sem, inc_k=16)
        P.wait_only("sync", [(sm_sem, 32 * NTC)])
        root.close()
        return nc, P

    for dst, src in ((convw_t, convw), (convb_t, convb), (bdt_t, bdt),
                     (acol, alog), (dpar_t, dpar)):
        ep = dma_in(dst[:], src, "par")

    G(lambda g: g.memset(ident[:], 0.0))
    G(lambda g: g.affine_select(
        out=ident[:], in_=ident[:], compare_op=ALU.not_equal, fill=1.0,
        base=0, pattern=[[-1, 128]], channel_multiplier=1))
    G(lambda g: g.memset(identb[:], 0.0))
    G(lambda g: g.affine_select(
        out=identb[:], in_=identb[:], compare_op=ALU.not_equal, fill=1.0,
        base=0, pattern=[[-1, 128]], channel_multiplier=1))
    e_setup = G(lambda g: g.memset(ones1[:], 1.0))

    e_ac = S(lambda e: e.activation(acol[:], acol[:], AF.Exp), wait=[ep])
    V(lambda e: e.tensor_scalar_mul(acol[:], acol[:], -1.0), wait=[e_ac])

    e_cl = dma_in(gsb[:], clip2[0:1, 0:1], "par")
    e_t = TE(lambda e: e.matmul(pM[:, 0:1], ones1[:], gsb[:],
                                start=True, stop=True), wait=[e_cl, e_setup])
    e_s = S(lambda e: e.activation(clip_t[:, 0:1], pM[:, 0:1], AF.Sigmoid),
            wait=[e_t])
    e_cl = dma_in(gsb[:], clip2[0:1, 1:2], "par", wait=[e_t])
    e_t = TE(lambda e: e.matmul(pM[:, 0:1], ones1[:], gsb[:],
                                start=True, stop=True), wait=[e_cl, e_s])
    e_clipd = S(lambda e: e.activation(clip_t[:, 1:2], pM[:, 0:1], AF.Sigmoid),
                wait=[e_t])
    setlu(pM, e_clipd)
    setlu(gsb, e_t)

    def ln_stats_tile(tile, tc, rstd_c, nmr_c, with_max, lmx, w0=()):
        e_sqr = S(lambda e, t=tile, tc=tc: e.activation(
            qa[:], t[:], AF.Square, accum_out=sqS[:, tc:tc + 1]),
            wait=lu(qa) + list(w0))
        setlu(qa, e_sqr)
        V(lambda e, t=tile, tc=tc: e.reduce_sum(sumS[:, tc:tc + 1], t[:], AX),
          wait=list(w0))
        V(lambda e, tc=tc: e.tensor_scalar_mul(
            sumS[:, tc:tc + 1], sumS[:, tc:tc + 1], 1.0 / H))
        V(lambda e, tc=tc: e.tensor_tensor(
            col1[:], sumS[:, tc:tc + 1], sumS[:, tc:tc + 1], ALU.mult))
        e_v = V(lambda e, tc=tc: e.scalar_tensor_tensor(
            col1[:], sqS[:, tc:tc + 1], 1.0 / H, col1[:],
            ALU.mult, ALU.subtract), wait=[e_sqr])
        e_v = V(lambda e: e.tensor_scalar(col1[:], col1[:], EPS, None,
                                          ALU.add), wait=[e_v])
        e_sq = S(lambda e: e.activation(col2[:], col1[:], AF.Sqrt),
                 wait=[e_v])
        V(lambda e, r=rstd_c, tc=tc: e.reciprocal(r[:, tc:tc + 1], col2[:]),
          wait=[e_sq])
        e_n = V(lambda e, r=rstd_c, m=nmr_c, tc=tc: e.scalar_tensor_tensor(
            m[:, tc:tc + 1], sumS[:, tc:tc + 1], -1.0, r[:, tc:tc + 1],
            ALU.mult, ALU.mult))
        if with_max:
            e_a = S(lambda e, t=tile, r=rstd_c, m=nmr_c, tc=tc: e.activation(
                qa[:], t[:], AF.Identity, bias=m[:, tc:tc + 1],
                scale=r[:, tc:tc + 1]), wait=[e_n] + lu(qa))
            e_n = V(lambda e, tc=tc, lx=lmx: e.reduce_max(
                lx[:, tc:tc + 1], qa[:], AX, apply_absolute_value=True),
                wait=[e_a])
            setlu(qa, e_n)
        return e_n

    def global_max(k, ncols, s_rep, is_rep, sig_idx, lmx):
        e1 = V(lambda e, n=ncols, lx=lmx: e.reduce_max(col1[:], lx[:, 0:n], AX))
        e2 = TE(lambda e: e.transpose(pT1[0:1, 0:128], col1[:], ident[:]),
                wait=[e1] + lu(pT1))
        e3 = V(lambda e: e.tensor_copy(grow[:], pT1[0:1, 0:128]), wait=[e2])
        setlu(pT1, e3)
        e4 = V(lambda e: e.reduce_max(gsb[:], grow[:], AX),
               wait=[e3] + lu(gsb))
        e5 = dma_out(gmi[k], gsb[:], "gm", wait=[e4])
        P.add("gpsimd", lambda e, k=k: e.collective_compute(
            "AllReduce", ALU.max, replica_groups=RG,
            ins=[gmi[k].opt()], outs=[gmo[k].opt()]), wait=[e5], inc=CC)
        e6 = (CC, CC.v)
        e7 = dma_in(gsb[:], gmo[k], "gm", wait=[e6])
        setlu(gsb, e7)
        e8 = TE(lambda e: e.matmul(pM[:, 0:1], ones1[:], gsb[:],
                                   start=True, stop=True),
                wait=[e7] + lu(pM))
        setlu(gsb, e8)
        e9 = V(lambda e, si=sig_idx, sr=s_rep: e.scalar_tensor_tensor(
            sr[:], pM[:, 0:1], 1.0 / QMAX, clip_t[:, si:si + 1],
            ALU.mult, ALU.mult), wait=[e8, e_clipd])
        setlu(pM, e9)
        e10 = V(lambda e, sr=s_rep, ir=is_rep: e.reciprocal(ir[:], sr[:]),
                wait=[e9])
        return e10

    stk0 = contextlib.ExitStack()
    wide_f = [mk(stk0, f"widef{i}", [128, II]) for i in range(2)]
    wide_b = [mk(stk0, f"wideb{i}", [128, II], BF16) for i in range(2)]

    def cast_spill(src, dst, nrows, cols):
        pieces = [(c0, min(cols, c0 + 2048)) for c0 in range(0, cols, 2048)]
        for r in range(nrows // 128):
            for c0, c1 in pieces:
                fs, bs = wide_f[r % 2], wide_b[r % 2]
                el = dma_in(fs[:, 0:c1 - c0],
                            src[r * 128:(r + 1) * 128, c0:c1],
                            "wf%d" % (r % 2), wait=lu(fs))
                ec = V(lambda e, f=fs, b=bs, c=c1 - c0: e.tensor_copy(
                    b[:, 0:c], f[:, 0:c]), wait=[el] + lu(bs))
                setlu(fs, ec)
                es = dma_out(dst[r * 128:(r + 1) * 128, c0:c1],
                             bs[:, 0:c1 - c0], "wb%d" % (r % 2), wait=[ec])
                setlu(bs, es)

    cast_spill(winT2, s_winbf2, DXZ, H)
    cast_spill(woutT, s_woutbf, D, H)
    el = dma_in(wide_f[0][:, 0:NDC * 96], wx_r, "wf0", wait=lu(wide_f[0]))
    ec = V(lambda e: e.tensor_copy(wx_b[:], wide_f[0][:, 0:NDC * 96]),
           wait=[el])
    setlu(wide_f[0], ec)
    el = dma_in(wide_f[1][0:RNK, 0:D], wdtT, "wf1", wait=lu(wide_f[1]))
    ec = V(lambda e: e.tensor_copy(wdt_b[:], wide_f[1][0:RNK, 0:D]), wait=[el])
    setlu(wide_f[1], ec)

    rq_row = mk(stk0, "rq_row", [1, NIC * 128])
    rqT_sb = mk(stk0, "rqT_sb", [NIC, 128])
    rq_bc = mk(stk0, "rq_bc", [128, II])
    amax_c = mk(stk0, "amax_c", [128, NIC])

    wq_pro = mk(stk0, "wq_pro", [128, II], BF16)

    def quant_w(w_nat, w_T, NRC, ncols, scale_col, spill, hook=None):
        for ic in range(NRC):
            fs = wide_f[ic % 2]
            el = dma_in(fs[:, 0:ncols], w_nat[ic * 128:(ic + 1) * 128, :],
                        "wf%d" % (ic % 2), wait=lu(fs))
            if hook is not None and ic < NTC:
                hook(ic)
            ec = V(lambda e, f=fs, ic=ic, nc_=ncols: e.reduce_max(
                amax_c[:, ic:ic + 1], f[:, 0:nc_], AX,
                apply_absolute_value=True), wait=[el])
            setlu(fs, ec)
        V(lambda e, n=NRC: e.tensor_scalar_mul(scale_col[:, 0:n],
                                               amax_c[:, 0:n], 1.0 / QMAX))
        V(lambda e, n=NRC: e.reciprocal(amax_c[:, 0:n], amax_c[:, 0:n]))
        e_rq = V(lambda e, n=NRC: e.tensor_scalar_mul(amax_c[:, 0:n],
                                                      amax_c[:, 0:n], QMAX))
        e_t = TE(lambda e, n=NRC: e.transpose(pT1[0:n, 0:128],
                                              amax_c[:, 0:n], ident[:]),
                 wait=[e_rq, e_setup] + lu(pT1))
        e_v = V(lambda e, n=NRC: e.tensor_copy(rqT_sb[0:n, :],
                                               pT1[0:n, 0:128]),
                wait=[e_t] + lu(rqT_sb))
        setlu(pT1, e_v)
        e_d = dma_out(rq_row[0:1, 0:NRC * 128], rqT_sb[0:NRC, :], "rq",
                      wait=[e_v] + lu(rq_row))
        setlu(rqT_sb, e_d)
        wid = NRC * 128
        e_bc = None
        for j in range((wid + 511) // 512):
            c0, c1 = j * 512, min(wid, (j + 1) * 512)
            e_t = TE(lambda e, c0=c0, c1=c1: e.matmul(
                pM[:, 0:c1 - c0], ones1[:], rq_row[0:1, c0:c1],
                start=True, stop=True), wait=[e_d] + lu(pM))
            setlu(rq_row, e_t)
            e_bc = V(lambda e, c0=c0, c1=c1: e.tensor_copy(
                rq_bc[:, c0:c1], pM[:, 0:c1 - c0]), wait=[e_t])
            setlu(pM, e_bc)
        for hc in range(ncols // 128):
            fs = wide_f[hc % 2]
            el = dma_in(fs[:, 0:wid], w_T[hc * 128:(hc + 1) * 128, 0:wid],
                        "wf%d" % (hc % 2), wait=lu(fs))
            V(lambda e, f=fs, w=wid: e.tensor_tensor(
                f[:, 0:w], f[:, 0:w], rq_bc[:, 0:w], ALU.mult),
                wait=[el, e_bc])
            V(lambda e, f=fs, w=wid: e.tensor_scalar(
                f[:, 0:w], f[:, 0:w], MAGIC, MAGIC, ALU.add, ALU.subtract))
            ec = V(lambda e, f=fs, w=wid: e.tensor_copy(
                wq_pro[:, 0:w], f[:, 0:w]), wait=lu(wq_pro))
            setlu(fs, ec)
            es = dma_out(spill[hc * 128:(hc + 1) * 128, 0:wid],
                         wq_pro[:, 0:wid], "wq", wait=[ec])
            setlu(wq_pro, es)

    def ln_tile(tc):
        tile = xt01[tc % 2]
        el = dma_in(tile[:], x_in[tc * 128:(tc + 1) * 128, :],
                    "x%d" % (tc % 2), wait=lu(tile), eng="scalar")
        e_n0 = ln_stats_tile(tile, tc, rstdA, nmrA, True, lmax, w0=[el])
        setlu(tile, e_n0)

    quant_w(wg_n, wgT, NIC, H, sg_col, s_wgq,
            hook=(ln_tile if stop != "smoke" else None))
    quant_w(wu_n, wuT, NIC, H, su_col, s_wuq)
    e_spills_gu = (DS("owq"), DS("owq").v)
    quant_w(wd_n, wdT, NHC, II, sd_col, s_wdq)
    e_spills_all = (DS("owq"), DS("owq").v)
    barrier()
    stk0.close()
    if stop == "pro":
        P.wait_only("sync", [(c, c.v) for c in ALLC])
        root.close()
        return nc, P

    ffn_s = contextlib.ExitStack()
    xq = mk(ffn_s, "xq", [128, NHC * T], BF16)
    xqt1 = [mk(ffn_s, f"xqt1_{i}", [128, H], BF16) for i in range(2)]
    stg1 = [mk(ffn_s, f"stg1_{i}", [128, TCF], BF16) for i in range(2)]
    xtf1 = xt01 + [mk(ffn_s, f"xtf1_{i}", [128, H]) for i in range(2)]
    wrkB1 = mk(ffn_s, "wrkB1", [128, TCF])
    wrkC1 = mk(ffn_s, "wrkC1", [128, TCF])
    wrkD1 = mk(ffn_s, "wrkD1", [128, TCF])
    pBF1 = mkp(ffn_s, "pBF1", [128, 2048], BF16)
    pT21 = mkp(ffn_s, "pT21", [128, 512], F32)

    def load_wgu(stk, tag):
        wgq_ = mk(stk, "wgq" + tag, [128, NHC * II], BF16)
        wuq_ = mk(stk, "wuq" + tag, [128, NHC * II], BF16)
        e_wl = None
        for hc in range(NHC):
            dma_in(wgq_[:, hc * II:(hc + 1) * II],
                   s_wgq[hc * 128:(hc + 1) * 128, :], "lg",
                   wait=[e_spills_gu] if hc == 0 else ())
            e_wl0 = dma_in(wuq_[:, hc * II:(hc + 1) * II],
                           s_wuq[hc * 128:(hc + 1) * 128, :], "lu",
                           wait=[e_spills_gu] if hc == 0 else ())
            e_wl = (DS("ilg"), DS("ilg").v) if False else e_wl0
        P.wait_only("tensor", [e_wl, (DS("ilg"), DS("ilg").v)])
        return wgq_, wuq_

    wstk1 = contextlib.ExitStack()
    wgq, wuq = load_wgu(wstk1, "1")

    def ffn_front(x_src, k0, k1, stats, wgq_, wuq_, xq_, xqt, pBF,
                  wrkB, wrkC, wrkD):
        bank_ev = {}
        rstd_c, nmr_c = stats
        if rstd_c is None:
            rstd_c, nmr_c = rstdA, nmrA
            for tc in range(NTC):
                tile = xt01[tc % 2]
                el = dma_in(tile[:], x_src[tc * 128:(tc + 1) * 128, :],
                            "x%d" % (tc % 2), wait=lu(tile), eng="scalar")
                P.wait_only("scalar", [el])
                P.wait_only("vector", [el])
                e_n = ln_stats_tile(tile, tc, rstd_c, nmr_c, True, lmax)
                setlu(tile, e_n)
        if stop == "stats":
            return None, None
        e_is = global_max(k0, NTC, sx_rep, isx_rep, 0, lmax)
        if stop == "gmax":
            return None, None
        V(lambda e: e.tensor_scalar(sxg_col[:, 0:NIC], sg_col[:, 0:NIC],
                                    sx_rep[:, 0:1], None, ALU.mult),
          wait=[e_is])
        V(lambda e: e.tensor_scalar(sxu_col[:, 0:NIC], su_col[:, 0:NIC],
                                    sx_rep[:, 0:1], None, ALU.mult))
        V(lambda e, r=rstd_c: e.tensor_scalar(
            scS[:, 0:NTC], r[:, 0:NTC], isx_rep[:, 0:1], None, ALU.mult))
        e_sc = V(lambda e, m=nmr_c: e.tensor_scalar(
            biS[:, 0:NTC], m[:, 0:NTC], isx_rep[:, 0:1], None, ALU.mult))
        for tc in range(NTC):
            tile = xt01[tc % 2]
            bt = xqt[tc % 2]
            el = dma_in(tile[:], x_src[tc * 128:(tc + 1) * 128, :],
                        "x%d" % (tc % 2), wait=lu(tile), eng="scalar")
            e_a = S(lambda e, t=tile, tc=tc: e.activation(
                qa[:], t[:], AF.Identity, bias=biS[:, tc:tc + 1],
                scale=scS[:, tc:tc + 1]), wait=[el, e_sc] + lu(qa))
            setlu(tile, e_a)
            e_m = V(lambda e: e.tensor_scalar(qa[:], qa[:], MAGIC, MAGIC,
                                              ALU.add, ALU.subtract),
                    wait=[e_a] + lu(bt))
            e_q = V(lambda e, b=bt: e.tensor_scalar(
                b[:], qa[:], -QMAX - 1.0, QMAX, ALU.max, ALU.min))
            setlu(qa, e_q)
            bank = (tc % 2) * 1024
            e_t = None
            for hc in range(NHC):
                wv = [e_q]
                if hc == 0 and bank in bank_ev:
                    wv.append(bank_ev[bank])
                e_t = TE(lambda e, b=bt, hc=hc, sl=bank + hc * 128: e.transpose(
                    pBF[:, sl:sl + 128], b[:, hc * 128:(hc + 1) * 128],
                    identb[:]), wait=wv)
            e_c = None
            for hc in range(NHC):
                e_c = V(lambda e, hc=hc, tc=tc, sl=bank + hc * 128, xq_=xq_:
                        e.tensor_copy(
                    xq_[:, hc * T + tc * 128:hc * T + (tc + 1) * 128],
                    pBF[:, sl:sl + 128]), wait=[e_t] if hc == 0 else ())
            bank_ev[bank] = e_c
            setlu(bt, e_c)
        e_xq = (CV, CV.v)
        if stop == "quant":
            return None, None
        V(lambda e: e.memset(lmax[:, 0:NIC], 0.0))
        e_sl = None
        e_b = None
        k = 0
        for tch in range(NCF):
            for ic in range(NIC):
                pg = [pA, pM][k % 2]
                pu = [pB, pT1][k % 2]
                wa = [wrkA, wrkC][k % 2]
                wb = [wrkB, wrkD][k % 2]
                k += 1
                wv = ([e_xq] if (tch == 0 and ic == 0) else []) + lu(pg)
                e_mg = None
                for hc in range(NHC):
                    e_mg = TE(lambda e, ic=ic, hc=hc, tch=tch,
                              w_=wgq_, x_=xq_, pg=pg: e.matmul(
                        pg[:, 0:512],
                        w_[:, hc * II + ic * 128:hc * II + (ic + 1) * 128],
                        x_[:, hc * T + tch * TCF:hc * T + tch * TCF + TCF],
                        start=(hc == 0), stop=(hc == NHC - 1)),
                        wait=wv if hc == 0 else ())
                wv = lu(pu)
                e_mu = None
                for hc in range(NHC):
                    e_mu = TE(lambda e, ic=ic, hc=hc, tch=tch,
                              w_=wuq_, x_=xq_, pu=pu: e.matmul(
                        pu[:, 0:512],
                        w_[:, hc * II + ic * 128:hc * II + (ic + 1) * 128],
                        x_[:, hc * T + tch * TCF:hc * T + tch * TCF + TCF],
                        start=(hc == 0), stop=(hc == NHC - 1)),
                        wait=wv if hc == 0 else ())
                e_sl = S(lambda e, ic=ic, pg=pg, wa=wa: e.activation(
                    wa[:], pg[:, 0:512], AF.Silu,
                    scale=sxg_col[:, ic:ic + 1]), wait=[e_mg] + lu(wa))
                setlu(pg, e_sl)
                e_b = V(lambda e, pu=pu, wa=wa, wb=wb: e.tensor_tensor(
                    wb[:], wa[:], pu[:, 0:512], ALU.mult),
                    wait=[e_sl, e_mu] + lu(wb))
                setlu(pu, e_b)
                setlu(wa, e_b)
                V(lambda e, wb=wb: e.reduce_max(col1[:], wb[:], AX,
                                                apply_absolute_value=True))
                V(lambda e, ic=ic: e.tensor_tensor(
                    lmax[:, ic:ic + 1], lmax[:, ic:ic + 1], col1[:], ALU.max))
                e_sp = dma_out(s_h[ic, tch], wb[:],
                               "sh%d" % (k % 2), wait=[e_b])
                setlu(wb, e_sp)
        if stop == "gateup":
            return None, None
        e_l = V(lambda e: e.tensor_tensor(lmax[:, 0:NIC], lmax[:, 0:NIC],
                                          sxu_col[:, 0:NIC], ALU.mult))
        e_ish = global_max(k1, NIC, sh_rep, ish_rep, 1, lmax)
        e_hq = V(lambda e: e.tensor_scalar(hq_col[:, 0:NIC], sxu_col[:, 0:NIC],
                                           ish_rep[:, 0:1], None, ALU.mult),
                 wait=[e_ish])
        e_dn = V(lambda e: e.tensor_scalar(
            dn_col[:, 0:NHC], sd_col[:, 0:NHC], 0.5, sh_rep[:, 0:1],
            ALU.mult, ALU.mult), wait=[e_ish])
        return e_hq, e_dn

    def ffn_down(x_src, dst, e_hq, e_dn, final, stk, xqt, stg_b, pBF,
                 pT2, xt):
        bank_ev = {}
        tag = "F" if final else "A"
        wdq = mk(stk, "wdq" + tag, [128, NIC * H], BF16)
        hq_sb2 = [mk(stk, f"hq_sb{tag}{i}", [128, NIC * TCF], BF16)
                  for i in range(2)]
        hstg = [mk(stk, f"hstg{tag}{i}", [128, TCF]) for i in range(2)]
        e_wd = None
        for ic in range(NIC):
            e_wd = dma_in(wdq[:, ic * H:(ic + 1) * H],
                          s_wdq[ic * 128:(ic + 1) * 128, :], "wd",
                          wait=[e_spills_all] if ic == 0 else ())
        e_add_prev = None
        hqs_ev = {}

        def q_stage(tch):
            hq_sb = hq_sb2[tch % 2]
            e_hqs = None
            for ic in range(NIC):
                hs = hstg[ic % 2]
                el = dma_in(hs[:], s_h[ic, tch],
                            "h%d" % (ic % 2), wait=lu(hs), eng="scalar")
                V(lambda e, h=hs, ic=ic: e.tensor_scalar(
                    h[:], h[:], hq_col[:, ic:ic + 1], MAGIC, ALU.mult,
                    ALU.add), wait=[el, e_hq])
                V(lambda e, h=hs: e.tensor_scalar(
                    h[:], h[:], MAGIC, None, ALU.subtract))
                e_hqs = V(lambda e, h=hs, ic=ic, hq_sb=hq_sb: e.tensor_scalar(
                    hq_sb[:, ic * TCF:(ic + 1) * TCF], h[:],
                    -QMAX - 1.0, QMAX, ALU.max, ALU.min),
                    wait=lu(hq_sb) if ic == 0 else ())
                setlu(hs, e_hqs)
            hqs_ev[tch] = e_hqs

        q_stage(0)
        for tch in range(NCF):
            if stop in ("downq", "downs", "downm", "downm1", "downm2",
                        "downo", "downmA", "downmB") and tch == 1:
                return
            hq_sb = hq_sb2[tch % 2]
            e_hqs = hqs_ev[tch]
            if tch + 1 < NCF:
                q_stage(tch + 1)
            exs = []
            for sub in range(4):
                tc = tch * 4 + sub
                exs.append(dma_in(xt[sub][:],
                                  x_src[tc * 128:(tc + 1) * 128, :],
                                  "x%d" % sub, wait=lu(xt[sub]),
                                  eng="scalar"))
            nhc_run = NHC
            if stop in ("downq", "downs"):
                nhc_run = 0
            elif stop in ("downm1", "downmA", "downmB"):
                nhc_run = 1
            elif stop == "downm2":
                nhc_run = 2
            for hc in range(nhc_run):
                pd = [pA, pM][hc % 2]
                e_m = None
                for ic in range(NIC):
                    wv = []
                    if ic == 0:
                        wv = [e_hqs, e_wd] + lu(pd)
                        if e_add_prev is not None and hc == 0:
                            wv.append(e_add_prev)
                    e_m = TE(lambda e, ic=ic, hc=hc, pd=pd, hq_sb=hq_sb:
                             e.matmul(
                        pd[:, 0:512],
                        wdq[:, ic * H + hc * 128:ic * H + (hc + 1) * 128],
                        hq_sb[:, ic * TCF:(ic + 1) * TCF],
                        start=(ic == 0), stop=(ic == NIC - 1)), wait=wv)
                if stop == "downmA":
                    e_add_prev = e_m
                    continue
                setlu(hq_sb, e_m)
                e_e = S(lambda e, hc=hc, pd=pd: e.activation(
                    wrkA[:], pd[:, 0:512], AF.Identity,
                    scale=dn_col[:, hc:hc + 1]),
                    wait=[e_m, e_dn] + lu(wrkA))
                setlu(pd, e_e)
                if stop == "downmB":
                    setlu(wrkA, e_e)
                    e_add_prev = e_e
                    continue
                e_tl = None
                for sub in range(4):
                    pb = pT1 if sub % 2 == 0 else pT2
                    sl = (sub // 2) * 128
                    key = ("dn", sub % 2)
                    wv = [e_e, exs[sub]]
                    if sub < 2 and key in bank_ev:
                        wv.append(bank_ev[key])
                    e_tl = TE(lambda e, pb=pb, sl=sl, sub=sub: e.transpose(
                        pb[:, sl:sl + 128],
                        wrkA[:, sub * 128:(sub + 1) * 128], ident[:]),
                        wait=wv)
                e_add = None
                for sub in range(4):
                    pb = pT1 if sub % 2 == 0 else pT2
                    sl = (sub // 2) * 128
                    e_add = V(lambda e, pb=pb, sl=sl, sub=sub, hc=hc:
                              e.tensor_tensor(
                        xt[sub][:, hc * 128:(hc + 1) * 128],
                        xt[sub][:, hc * 128:(hc + 1) * 128],
                        pb[:, sl:sl + 128], ALU.add),
                        wait=[e_tl] if sub == 0 else ())
                    bank_ev[("dn", sub % 2)] = e_add
                setlu(wrkA, e_tl)
                e_add_prev = e_add
            if stop == "downq":
                continue
            if stop == "downs" and e_add_prev is None:
                e_add_prev = (CV, CV.v)
            for sub in range(4):
                tc = tch * 4 + sub
                e_o = dma_out(dst[tc * 128:(tc + 1) * 128, :], xt[sub][:],
                              "sx%d" % sub, wait=[e_add_prev], eng="scalar")
                setlu(xt[sub], e_o)
                if not final and not str(stop).startswith("downm"):
                    P.wait_only("scalar", [e_add_prev])
                    e_n = ln_stats_tile(xt[sub], tc, rstdB, nmrB, False, None)
                    bt = xqt[sub % 2]
                    e_a = S(lambda e, sub=sub, tc=tc, b=bt: e.activation(
                        b[:], xt[sub][:], AF.Identity,
                        bias=nmrB[:, tc:tc + 1], scale=rstdB[:, tc:tc + 1]),
                        wait=[e_n] + lu(bt))
                    addlu(xt[sub], e_a)
                    bank = (sub % 2) * 1024
                    e_t = None
                    for hc in range(NHC):
                        wv = [e_a]
                        if hc == 0 and ("x", bank) in bank_ev:
                            wv.append(bank_ev[("x", bank)])
                        e_t = TE(lambda e, b=bt, hc=hc, sl=bank + hc * 128:
                                 e.transpose(
                            pBF[:, sl:sl + 128],
                            b[:, hc * 128:(hc + 1) * 128], identb[:]),
                            wait=wv)
                    e_c = None
                    for hc in range(NHC):
                        sg = stg_b[hc % 2]
                        e_c = V(lambda e, sl=bank + hc * 128, sg=sg:
                                e.tensor_copy(
                            sg[:, 0:128], pBF[:, sl:sl + 128]),
                            wait=([e_t] if hc == 0 else []) + lu(sg))
                        es = dma_out(
                            s_xln2[hc, tc],
                            sg[:, 0:128], "xl%d" % (hc % 2), wait=[e_c],
                            eng="scalar")
                        setlu(sg, es)
                    bank_ev[("x", bank)] = e_c
                    setlu(bt, e_c)

    e_hq, e_dn = ffn_front(x_in, 0, 1, (rstdA, nmrA), wgq, wuq, xq,
                           xqt1, pBF1, wrkB1, wrkC1, wrkD1)
    barrier()
    if stop in ("front", "stats", "gmax", "quant", "gateup"):
        P.wait_only("sync", [(c, c.v) for c in ALLC])
        wstk1.close()
        ffn_s.close()
        root.close()
        return nc, P
    wstk1.close()
    dstk = contextlib.ExitStack()
    ffn_down(x_in, o_out if stop == "ffn1" else s_x1, e_hq, e_dn,
             final=(stop == "ffn1"), stk=dstk, xqt=xqt1, stg_b=stg1,
             pBF=pBF1, pT2=pT21, xt=xtf1)
    barrier()
    dstk.close()
    ffn_s.close()
    if stop in ("ffn1", "downq", "downs", "downm", "downm1", "downm2", "downo", "downmA", "downmB"):
        P.wait_only("sync", [(c, c.v) for c in ALLC])
        root.close()
        return nc, P

    mam = contextlib.ExitStack()
    selBC = mk(mam, "selBC", [16, NST * 128])
    bcC = mk(mam, "bcC", [16, TCM])
    dA = [mk(mam, f"dA{i}", [128, WSC]) for i in range(2)]
    dBu2 = [mk(mam, f"dBu{i}", [128, WSC]) for i in range(2)]
    stt = mk(mam, "stt", [128, WSC])
    Ball = mk(mam, "Ball", [128, WSC])
    Call = mk(mam, "Call", [128, WSC])
    xp = [mk(mam, f"xp{i}", [128, TCM + 3]) for i in range(2)]
    tails = mk(mam, "tails", [128, NDC * 3])
    u_t4 = [mk(mam, f"u{i}", [128, TCM]) for i in range(NDC)]
    sz_t = [mk(mam, f"sz{i}", [128, TCM]) for i in range(NDC)]
    ubf = [mk(mam, f"ubf{i}", [128, TCM], BF16) for i in range(2)]
    dl_t = [mk(mam, f"dl{i}", [128, TCM]) for i in range(2)]
    du_t = [mk(mam, f"du{i}", [128, TCM]) for i in range(2)]
    dltb = mk(mam, "dltb", [RNK, TCM], BF16)
    bc_sb = mk(mam, "bc_sb", [32, TCM])
    carry = [mk(mam, f"cr{i}", [128, NST]) for i in range(NDC)]
    ytmp = mk(mam, "ytmp", [128, TCM])
    ybf = [mk(mam, f"ybf{i}", [128, TCM], BF16) for i in range(2)]
    win_g2 = [mk(mam, f"wing{i}", [128, H], BF16) for i in range(2)]
    xl_c = [mk(mam, f"xl{i}", [128, TCM], BF16) for i in range(NHC)]
    wo_c = [mk(mam, f"wo{i}", [128, H], BF16) for i in range(2)]
    wk1p = [mk(mam, f"wk1_{i}", [128, TCM]) for i in range(2)]
    wk2m1 = mk(mam, "wk2m", [128, TCM])
    wk2mp = [wk2m1, wk2m1]
    pW = mkp(mam, "pW", [128, 2048], F32)

    G(lambda g: g.memset(selBC[:], 0.0))
    G(lambda g: g.affine_select(
        out=selBC[:], in_=selBC[:],
        compare_op=ALU.not_equal, fill=1.0, base=0,
        pattern=[[-1, NST], [0, 128]], channel_multiplier=1))
    G(lambda g: g.memset(dA[0][:], 0.0))
    G(lambda g: g.memset(dA[1][:], 0.0))
    G(lambda g: g.memset(Ball[:], 0.0))
    G(lambda g: g.memset(Call[:], 0.0))
    for dc in range(NDC):
        G(lambda g, dc=dc: g.memset(carry[dc][:], 0.0))
    G(lambda g: g.memset(tails[:], 0.0))
    e_ms = G(lambda g: g.memset(ytmp[:], 0.0))

    def seg(t, n):
        return t[:, n * GAP + 1:(n + 1) * GAP]

    def gapped3(t):
        return t[:].rearrange("p (n g) -> p n g", n=NST)[:, :, 1:]

    def bounds3(t):
        return t[:].rearrange("p (n g) -> p n g", n=NST)[:, :, 0:1]

    e_x2w = []
    sz_pe = {}
    treev = {}
    for c in range(NCM):
        e_w = {}
        for hc in range(NHC):
            dma_in(xl_c[hc][:, 0:128], s_xln2[hc, 2 * c],
                   "mx%d" % (hc % 4), wait=lu(xl_c[hc]))
            e_w[hc] = dma_in(xl_c[hc][:, 128:256], s_xln2[hc, 2 * c + 1],
                             "mx%d" % (hc % 4))
        pps = [pA, pB]
        e_xp = {}
        e_u = {}
        e_xdbl = None
        cv_ev = {}

        def u_stage(dc):
            wk1 = wk1p[dc % 2]
            e_uu = S(lambda e, dc=dc, wk1=wk1: e.activation(
                u_t4[dc][:], wk1[:], AF.Silu,
                bias=convb_t[:, dc:dc + 1]),
                wait=[cv_ev[dc]] + lu(u_t4[dc]))
            setlu(wk1, e_uu)
            e_u[dc] = e_uu
            e_ub = S(lambda e, dc=dc: e.activation(
                ubf[dc % 2][:], u_t4[dc][:], AF.Identity),
                wait=lu(ubf[dc % 2]))
            wv2 = [e_ub] + (lu(pM) if dc == 0 else [])
            e_x = TE(lambda e, dc=dc: e.matmul(
                pM[0:96, 0:TCM], wx_b[:, dc * 96:(dc + 1) * 96],
                ubf[dc % 2][:], start=(dc == 0), stop=(dc == NDC - 1)),
                wait=wv2)
            setlu(ubf[dc % 2], e_x)
            return e_x

        for g2 in range(DXZ // 128):
            pp = pps[g2 % 2]
            wt = win_g2[g2 % 2]
            e_wg = dma_in(wt[:], s_winbf2[g2 * 128:(g2 + 1) * 128, :],
                          "wg%d" % (g2 % 2), wait=lu(wt))
            e_m = None
            for hc in range(NHC):
                wv = [e_wg] + ([e_w[hc]] if g2 == 0 else []) + \
                    (lu(pp) if hc == 0 else [])
                e_m = TE(lambda e, hc=hc, pp=pp, wt=wt: e.matmul(
                    pp[:, 0:TCM], wt[:, hc * 128:(hc + 1) * 128],
                    xl_c[hc][:], start=(hc == 0), stop=(hc == NHC - 1)),
                    wait=wv if (hc == 0 or g2 == 0) else [])
                setlu(xl_c[hc], e_m)
            setlu(wt, e_m)
            if g2 < NDC:
                dc = g2
                xpt = xp[dc % 2]
                e_mg = G(lambda e, dc=dc, x_=xpt: e.tensor_copy(
                    x_[:, 0:3], tails[:, dc * 3:dc * 3 + 3]),
                    wait=([e_ms] if c == 0 else []) + lu(xpt))
                e_c = S(lambda e, x_=xpt, pp=pp: e.activation(
                    x_[:, 3:TCM + 3], pp[:, 0:TCM], AF.Identity),
                    wait=[e_m, e_mg])
                e_xp[dc] = e_c
                def cwb(col):
                    return convw_t[:, col:col + 1].rearrange(
                        "p (o t) -> p o t", o=1).broadcast_to([128, 1, TCM])

                def v3(t):
                    return t.rearrange("p (o t) -> p o t", o=1)

                wk1 = wk1p[dc % 2]
                wk2m = wk2mp[dc % 2]
                e_cv = G(lambda e, dc=dc, x_=xpt, wk1=wk1: e.tensor_tensor(
                    v3(wk1[:]), v3(x_[:, 0:TCM]), cwb(dc * 4), ALU.mult),
                    wait=[e_c] + lu(wk1))
                for j in range(1, 4):
                    G(lambda e, dc=dc, j=j, x_=xpt, wk2m=wk2m: e.tensor_tensor(
                        v3(wk2m[:]), v3(x_[:, j:j + TCM]), cwb(dc * 4 + j),
                        ALU.mult), wait=lu(wk2m))
                    e_cv = G(lambda e, wk1=wk1, wk2m=wk2m: e.tensor_tensor(
                        wk1[:], wk1[:], wk2m[:], ALU.add))
                    setlu(wk2m, e_cv)
                e_tl = G(lambda e, dc=dc, x_=xpt: e.tensor_copy(
                    tails[:, dc * 3:dc * 3 + 3], x_[:, TCM:TCM + 3]),
                    wait=[e_c])
                setlu(xpt, e_tl)
                cv_ev[dc] = e_cv
                e_c = e_xp[dc]
            else:
                dc = g2 - NDC
                e_c = S(lambda e, dc=dc, pp=pp: e.activation(
                    sz_t[dc][:], pp[:, 0:TCM], AF.Silu),
                    wait=[e_m] + lu(sz_t[dc]))
                sz_pe[dc] = e_c
            setlu(pp, e_c)
            if 1 <= g2 <= NDC:
                e_xdbl = u_stage(g2 - 1)
        e_dlt = V(lambda e: e.tensor_copy(dltb[:], pM[0:RNK, 0:TCM]),
                  wait=[e_xdbl] + lu(dltb))
        e_bcv = V(lambda e: e.tensor_copy(bc_sb[:], pM[64:96, 0:TCM]),
                  wait=lu(bc_sb))
        setlu(pM, e_bcv)
        e_bcc = dma_in(bcC[:], bc_sb[16:32, :], "bcc", wait=[e_bcv] + lu(bcC),
                       eng="scalar")
        e_bev = None
        e_cev = None
        e_tb = None
        e_tc = None
        for n in range(NST):
            e_tb = TE(lambda e, n=n: e.matmul(
                pM[:, 0:TCM], selBC[0:16, n * 128:(n + 1) * 128],
                bc_sb[0:16, :], start=True, stop=True),
                wait=([e_bcv] if n == 0 else [e_bev]) + lu(pM))
            e_bev = S(lambda e, n=n: e.activation(
                seg(Ball, n), pM[:, 0:TCM], AF.Identity),
                wait=[e_tb] + (lu(Ball) if n == 0 else []))
            setlu(pM, e_bev)
            e_tc = TE(lambda e, n=n: e.matmul(
                pB[:, 0:TCM], selBC[0:16, n * 128:(n + 1) * 128],
                bcC[:], start=True, stop=True),
                wait=([e_bcc] + lu(pB) if n == 0 else [e_cev]))
            e_cev = V(lambda e, n=n: e.tensor_copy(
                seg(Call, n), pB[:, 0:TCM]),
                wait=[e_tc] + (lu(Call) if n == 0 else []))
        setlu(pB, e_cev)
        setlu(bc_sb, e_tb)
        setlu(bcC, e_tc)
        e_ball = e_bev
        e_call = e_cev
        ev = {}

        def stage_prep_sp(dc):
            pp = pps[dc % 2]
            dl = dl_t[dc % 2]
            du = du_t[dc % 2]
            e_m = TE(lambda e, dc=dc, pp=pp: e.matmul(
                pp[:, 0:TCM], wdt_b[:, dc * 128:(dc + 1) * 128], dltb[:],
                start=True, stop=True), wait=[e_dlt] + lu(pp))
            setlu(dltb, e_m)
            S(lambda e, dc=dc, pp=pp, du=du: e.activation(
                du[:], pp[:, 0:TCM], AF.Abs, bias=bdt_t[:, dc:dc + 1]),
                wait=[e_m] + lu(du))
            S(lambda e, du=du: e.activation(du[:], du[:], AF.Exp, scale=-1.0))
            e_s1 = S(lambda e, du=du: e.activation(du[:], du[:], AF.Ln,
                                                   bias=1.0))
            e_s2 = S(lambda e, dc=dc, pp=pp, dl=dl: e.activation(
                dl[:], pp[:, 0:TCM], AF.Relu, bias=bdt_t[:, dc:dc + 1]),
                wait=lu(dl))
            setlu(pp, e_s2)
            ev[("sp", dc)] = e_s2

        def stage_prep_v(dc):
            dl = dl_t[dc % 2]
            du = du_t[dc % 2]
            e_d = V(lambda e, dl=dl, du=du: e.tensor_tensor(
                dl[:], dl[:], du[:], ALU.add), wait=[ev[("sp", dc)]])
            e_du = V(lambda e, dl=dl, du=du, dc=dc: e.tensor_tensor(
                du[:], dl[:], u_t4[dc][:], ALU.mult),
                wait=[e_u[dc]])
            addlu(u_t4[dc], e_du)
            da = dA[dc % 2]
            e_e = None
            for n in range(NST):
                wv = [e_d] + (lu(da) if n == 0 else [])
                e_e = S(lambda e, dc=dc, n=n, dl=dl, da=da: e.activation(
                    seg(da, n), dl[:], AF.Exp,
                    scale=acol[:, dc * NST + n:dc * NST + n + 1]), wait=wv)
            setlu(dl, e_e)
            ev[("exp", dc)] = e_e
            ev[("du", dc)] = e_du

        def g3(t, a, b):
            return t[:, a * GAP:b * GAP].rearrange(
                "p (n g) -> p n g", n=b - a)[:, :, 1:]

        def stage_dbu_g(dc):
            du = du_t[dc % 2]
            dbu = dBu2[dc % 2]
            tv = treev.get(dc % 2, {})

            def du3(nn):
                return du[:].rearrange("p (o t) -> p o t",
                                       o=1).broadcast_to([128, nn, TCM])
            G(lambda e, dbu=dbu, du=du: e.tensor_tensor(
                g3(dbu, 8, 12), du3(4), g3(Ball, 8, 12), ALU.mult),
                wait=[ev[("du", dc)], e_ball])
            G(lambda e, dbu=dbu, du=du: e.tensor_tensor(
                g3(dbu, 12, 16), du3(4), g3(Ball, 12, 16), ALU.mult))
            G(lambda e, dbu=dbu, du=du: e.tensor_tensor(
                g3(dbu, 4, 8), du3(4), g3(Ball, 4, 8), ALU.mult),
                wait=[tv["lvl2"]] if "lvl2" in tv else [])
            e_lo = G(lambda e, dbu=dbu, du=du: e.tensor_tensor(
                g3(dbu, 0, 4), du3(4), g3(Ball, 0, 4), ALU.mult),
                wait=[tv["yadd"]] if "yadd" in tv else [])
            e_inj = G(lambda e, dc=dc, dbu=dbu: e.tensor_copy(
                bounds3(dbu),
                carry[dc][:].rearrange("p (n o) -> p n o", o=1)),
                wait=lu(carry[dc]))
            setlu(du, e_lo)
            addlu(Ball, e_lo)
            ev[("dbuL", dc)] = e_lo
            ev[("inj", dc)] = e_inj

        def stage_scan_v(dc):
            da = dA[dc % 2]
            dbu = dBu2[dc % 2]
            e_scan = V(lambda e, da=da, dbu=dbu: e.tensor_tensor_scan(
                stt[:], da[:], dbu[:], 0.0, ALU.mult, ALU.add),
                wait=[ev[("exp", dc)], ev[("dbuL", dc)], ev[("inj", dc)]] +
                lu(stt))
            setlu(da, e_scan)
            e_cx = S(lambda e, dc=dc: e.activation(
                carry[dc][:],
                stt[:].rearrange("p (n g) -> p n g", n=NST)[:, :, GAP - 1],
                AF.Identity), wait=[e_scan])
            setlu(carry[dc], e_cx)
            V(lambda e, dbu=dbu: e.tensor_tensor(
                dbu[:, 0:8 * GAP], stt[:, 0:8 * GAP],
                Call[:, 0:8 * GAP], ALU.mult), wait=[e_call])
            e_ym = V(lambda e, dbu=dbu: e.tensor_tensor(
                dbu[:, 8 * GAP:16 * GAP], stt[:, 8 * GAP:16 * GAP],
                Call[:, 8 * GAP:16 * GAP], ALU.mult))
            setlu(stt, e_ym)
            addlu(stt, e_cx)
            addlu(Call, e_ym)
            ev[("ym", dc)] = e_ym

        def stage_tail_g(dc):
            dbu = dBu2[dc % 2]
            ev[("lvl1", dc)] = G(lambda e, dbu=dbu: e.tensor_tensor(
                dbu[:, 0:8 * GAP], dbu[:, 0:8 * GAP],
                dbu[:, 8 * GAP:16 * GAP], ALU.add),
                wait=[ev[("ym", dc)]])

        def stage_tail_v(dc):
            dbu = dBu2[dc % 2]
            tre = {}
            e_tr = None
            for lev in (4, 2):
                e_tr = V(lambda e, lev=lev, dbu=dbu: e.tensor_tensor(
                    dbu[:, 0:lev * GAP], dbu[:, 0:lev * GAP],
                    dbu[:, lev * GAP:2 * lev * GAP], ALU.add),
                    wait=[ev[("lvl1", dc)]] if lev == 4 else ())
                if lev == 4:
                    tre["lvl2"] = e_tr
            e_y1 = V(lambda e, dc=dc, dbu=dbu: e.scalar_tensor_tensor(
                ytmp[:], u_t4[dc][:], dpar_t[:, dc:dc + 1],
                dbu[:, 1:GAP], ALU.mult, ALU.add), wait=[e_tr] + lu(ytmp))
            setlu(u_t4[dc], e_y1)
            e_y1 = V(lambda e, dbu=dbu: e.tensor_tensor(
                ytmp[:], ytmp[:], dbu[:, GAP + 1:2 * GAP], ALU.add))
            tre["yadd"] = e_y1
            treev[dc % 2] = tre
            e_y2 = V(lambda e, dc=dc: e.tensor_tensor(
                ytmp[:], ytmp[:], sz_t[dc][:], ALU.mult),
                wait=[sz_pe[dc]])
            setlu(sz_t[dc], e_y2)
            ev[("y2", dc)] = e_y2

        def stage_yb(dc):
            yb = ybf[dc % 2]
            e_yb = S(lambda e, yb=yb: e.activation(
                yb[:], ytmp[:], AF.Identity),
                wait=[ev[("y2", dc)]] + lu(yb))
            setlu(ytmp, e_yb)
            ev[("yb", dc)] = e_yb

        def stage_wout(dc):
            yb = ybf[dc % 2]
            wo = wo_c[dc % 2]
            e_wo = dma_in(wo[:], s_woutbf[dc * 128:(dc + 1) * 128, :],
                          "wo%d" % (dc % 2), wait=lu(wo))
            e_wm = None
            for hc in range(NHC):
                wv = [ev[("yb", dc)], e_wo]
                if dc == 0 and hc == 0:
                    wv = wv + lu(pW)
                e_wm = TE(lambda e, hc=hc, wo=wo, yb=yb, dc=dc: e.matmul(
                    pW[:, hc * TCM:(hc + 1) * TCM],
                    wo[:, hc * 128:(hc + 1) * 128], yb[:],
                    start=(dc == 0 and hc % 2 == 0),
                    stop=(dc == NDC - 1 and hc % 2 == 1)),
                    wait=wv if hc == 0 else [])
            setlu(wo, e_wm)
            setlu(yb, e_wm)
            ev[("wm", dc)] = e_wm

        stage_prep_sp(0)
        stage_prep_sp(1)
        stage_prep_v(0)
        stage_dbu_g(0)
        for dc in range(NDC):
            if dc >= 1:
                stage_tail_g(dc - 1)
            if dc + 1 < NDC:
                stage_prep_v(dc + 1)
            stage_scan_v(dc)
            if dc >= 1:
                stage_tail_v(dc - 1)
            if dc + 2 < NDC:
                stage_prep_sp(dc + 2)
            if dc >= 1:
                stage_yb(dc - 1)
            if dc + 1 < NDC:
                stage_dbu_g(dc + 1)
            if dc >= 1:
                stage_wout(dc - 1)
        stage_tail_g(NDC - 1)
        stage_tail_v(NDC - 1)
        stage_yb(NDC - 1)
        stage_wout(NDC - 1)
        e_wm = ev[("wm", NDC - 1)]
        exs = []
        for sub in range(2):
            tc = c * 2 + sub
            exs.append(dma_in(xt01[sub][:],
                              s_x1[tc * 128:(tc + 1) * 128, :],
                              "x%d" % sub, wait=lu(xt01[sub]), eng="scalar"))
        e_add = None
        add_ev = {}
        wrk_ev = {}
        for hc in range(NHC):
            par = hc % 2
            wof = par * TCM
            pb = pT1 if par == 0 else pB
            e_e = S(lambda e, hc=hc, wof=wof: e.activation(
                wrkA[:, wof:wof + TCM], pW[:, hc * TCM:(hc + 1) * TCM],
                AF.Identity),
                wait=[e_wm] + (lu(wrkA) if hc < 2 else [wrk_ev[par]]))
            e_t = None
            for sub in range(2):
                e_t = TE(lambda e, sub=sub, wof=wof, pb=pb: e.transpose(
                    pb[:, sub * 128:(sub + 1) * 128],
                    wrkA[:, wof + sub * 128:wof + (sub + 1) * 128],
                    ident[:]),
                    wait=[e_e, exs[sub]] +
                    ([add_ev[par]] if par in add_ev and sub == 0 else []) +
                    (lu(pb) if hc < 2 and sub == 0 else []))
            for sub in range(2):
                e_add = V(lambda e, sub=sub, hc=hc, pb=pb: e.tensor_tensor(
                    xt01[sub][:, hc * 128:(hc + 1) * 128],
                    xt01[sub][:, hc * 128:(hc + 1) * 128],
                    pb[:, sub * 128:(sub + 1) * 128], ALU.add),
                    wait=[e_t] if sub == 0 else ())
            add_ev[par] = e_add
            wrk_ev[par] = e_t
            setlu(pb, e_add)
        e_add = add_ev[1]
        setlu(wrkA, wrk_ev[0])
        addlu(wrkA, wrk_ev[1])
        setlu(pW, e_e)
        for sub in range(2):
            tc = c * 2 + sub
            dst2 = o_out if stop == "mamba" else s_x2
            e_o = dma_out(dst2[tc * 128:(tc + 1) * 128, :], xt01[sub][:],
                          "sx%d" % sub, wait=[e_add], eng="scalar")
            e_x2w.append(e_o)
            setlu(xt01[sub], e_o)
            if stop != "mamba":
                P.wait_only("scalar", [e_add])
                e_n = ln_stats_tile(xt01[sub], tc, rstdC, nmrC, True, lmax)
                addlu(xt01[sub], e_n)

    barrier()
    mam.close()
    if stop == "mamba":
        P.wait_only("sync", [(c, c.v) for c in ALLC])
        root.close()
        return nc, P

    ffn2 = contextlib.ExitStack()
    xq2 = mk(ffn2, "xq2", [128, NHC * T], BF16)
    xqt2 = [mk(ffn2, f"xqt2_{i}", [128, H], BF16) for i in range(2)]
    stg2 = [mk(ffn2, f"stg2_{i}", [128, TCF], BF16) for i in range(2)]
    xtf2 = xt01 + [mk(ffn2, f"xtf2_{i}", [128, H]) for i in range(2)]
    wrkB2 = mk(ffn2, "wrkB2", [128, TCF])
    wrkC2 = mk(ffn2, "wrkC2", [128, TCF])
    wrkD2 = mk(ffn2, "wrkD2", [128, TCF])
    pBF2 = mkp(ffn2, "pBF2", [128, 2048], BF16)
    pT22 = mkp(ffn2, "pT22", [128, 512], F32)
    wstk2 = contextlib.ExitStack()
    wgq2, wuq2 = load_wgu(wstk2, "2")
    e_hq2, e_dn2 = ffn_front(s_x2, 2, 3, (rstdC, nmrC), wgq2, wuq2,
                             xq2, xqt2, pBF2, wrkB2, wrkC2,
                             wrkD2)
    barrier()
    wstk2.close()
    dstk2 = contextlib.ExitStack()
    ffn_down(s_x2, o_out, e_hq2, e_dn2, final=True, stk=dstk2, xqt=xqt2,
             stg_b=stg2, pBF=pBF2, pT2=pT22, xt=xtf2)
    barrier()
    dstk2.close()
    ffn2.close()
    P.wait_only("sync", [(c, c.v) for c in ALLC])
    root.close()
    return nc, P


def _emit(nc, P):
    with nc.Block() as block:
        for eng in Prog.ENGINES:
            def section(e, eng=eng):
                for fn in P.streams[eng]:
                    fn(e)
            getattr(block, eng)(section)
    return nc


_NC_CACHE = {}


def _get_nc():
    key = _STOP
    if key not in _NC_CACHE:
        nc, P = build_nc(stop=_STOP)
        _NC_CACHE[key] = _emit(nc, P)
    return _NC_CACHE[key]


def kernel(L0_attn_x, L0_attn_W_in, L0_attn_conv_w, L0_attn_conv_b,
           L0_attn_W_x, L0_attn_W_dt, L0_attn_b_dt, L0_attn_A_log,
           L0_attn_D_param, L0_attn_W_out, L0_ffn_clip_x, L0_ffn_clip_down,
           L0_ffn_w_gate, L0_ffn_w_up, L0_ffn_w_down):
    def f32c(a):
        return np.ascontiguousarray(np.asarray(a, dtype=np.float32))

    def colmaj(v):
        return f32c(np.asarray(v).reshape(NDC, 128).T)

    x = f32c(L0_attn_x)
    shared = dict(
        winT2=f32c(np.asarray(L0_attn_W_in).reshape(32, 128, 8, 128)
                   .transpose(0, 3, 2, 1).reshape(DXZ, H)),
        wx_r=f32c(np.asarray(L0_attn_W_x).T.reshape(NDC, 128, 96)
                  .transpose(1, 0, 2).reshape(128, NDC * 96)),
        wdtT=f32c(np.asarray(L0_attn_W_dt).T),
        woutT=f32c(np.asarray(L0_attn_W_out).T),
        wg_n=f32c(L0_ffn_w_gate),
        wgT=f32c(np.asarray(L0_ffn_w_gate).T),
        wu_n=f32c(L0_ffn_w_up),
        wuT=f32c(np.asarray(L0_ffn_w_up).T),
        wd_n=f32c(L0_ffn_w_down),
        wdT=f32c(np.asarray(L0_ffn_w_down).T),
        convw=f32c(np.asarray(L0_attn_conv_w)[:, 0, :]
                   .reshape(NDC, 128, 4).transpose(1, 0, 2)
                   .reshape(128, NDC * 4)),
        convb=colmaj(L0_attn_conv_b),
        bdt=colmaj(L0_attn_b_dt),
        alog=f32c(np.asarray(L0_attn_A_log).reshape(NDC, 128, NST)
                  .transpose(1, 0, 2).reshape(128, NDC * NST)),
        dpar=colmaj(L0_attn_D_param),
        clip2=f32c(np.stack([np.asarray(L0_ffn_clip_x).ravel()[0],
                             np.asarray(L0_ffn_clip_down).ravel()[0]])
                   .reshape(1, 2)),
    )
    in_maps = [dict(shared, x=f32c(x[c])) for c in range(8)]
    nc = _get_nc()
    res = run_bass_kernel_spmd(nc, in_maps, core_ids=list(range(8)))
    return np.stack([np.asarray(res.results[c]["o"], dtype=np.float32)
                     for c in range(8)], axis=0)
